# version 12
# baseline (speedup 1.0000x reference)
import numpy as np
import ml_dtypes
from contextlib import ExitStack

import concourse.bass as bass
import concourse.mybir as mybir
from concourse.bass_utils import run_bass_kernel_spmd

F32 = mybir.dt.float32
F16 = mybir.dt.float16
BF16 = mybir.dt.bfloat16
I32 = mybir.dt.int32
U16 = mybir.dt.uint16
AF = mybir.ActivationFunctionType
ALU = mybir.AluOpType
AX = mybir.AxisListType.X

D = 2048
NH = 16
HD = 128
IDIM = 5632
GO = 2 * IDIM
GRP = 128
SCALE_QK = float(1.0 / np.sqrt(128.0))
CSHIFT = 10.0

ENGINES = ("sync", "gpsimd", "tensor", "vector", "scalar")


class Sem:
    def __init__(self, es, nc, name, lazy=False):
        self.h = es.enter_context(nc.semaphore(name))
        self.n = 0
        self.lazy = lazy
        self.refs = set()
        self.rank = None

    def ms(self):
        return (self, self.n)


class Plan:

    def __init__(self):
        self.ops = {e: [] for e in ENGINES}

    def op(self, eng, fn, waits=(), tag=None, inc=1):
        ws = [(s, v) for (s, v) in waits if s is not None and v > 0]
        ln = 0
        if tag is not None:
            tag.n += inc
            ln = tag.n
        self.ops[eng].append((ws, fn, tag, inc, ln))
        if tag is not None:
            return (tag, tag.n)
        return None

    def emit(self, block):
        lazy = set()
        for eng in ENGINES:
            for ws, fn, tag, inc, ln in self.ops[eng]:
                for s, v in ws:
                    if s.lazy:
                        s.refs.add(v)
                        lazy.add(s)
                if tag is not None and tag.lazy:
                    lazy.add(tag)
        for s in lazy:
            s.rank = {v: i + 1 for i, v in enumerate(sorted(s.refs))}
        for eng in ENGINES:
            lst = self.ops[eng]
            if not lst:
                continue

            def make(lst):
                def run(e):
                    for ws, fn, tag, inc, ln in lst:
                        for s, v in ws:
                            e.wait_ge(s.h, s.rank[v] if s.lazy else v)
                        if fn is None:
                            continue
                        ins = fn(e)
                        if tag is not None:
                            if tag.lazy:
                                if ln in tag.refs:
                                    ins.then_inc(tag.h, 1)
                            else:
                                ins.then_inc(tag.h, inc)
                return run
            getattr(block, eng)(make(lst))
        self.ops = {e: [] for e in ENGINES}


Z = (None, 0)


def build_program(NCORES, TC):
    S = NCORES * TC
    NT = TC // 128
    KT = S // 128
    QTW = min(512, TC)
    QT = TC // QTW
    NQS = QTW // 128
    NTH = max(1, TC // 512)
    TH = TC // NTH
    OSH = GO // NCORES
    DSH = D // NCORES
    NB_D = D // 512
    NB_I = IDIM // 512
    NCH_DN = 4 * NB_I
    DNW_U = IDIM // 4 // 128
    NSG = D // GRP
    NSD = IDIM // GRP
    groups = []
    k0 = 0
    while k0 < KT:
        ln = min(3, KT - k0)
        groups.append((k0, ln))
        k0 += ln
    NG = len(groups)
    NITS = NH * QT
    VSPL = max(1, KT // 16)
    KT_PER_V = (KT + VSPL - 1) // VSPL

    nc = bass.Bass("TRN2", target_bir_lowering=False)

    x_d = nc.dram_tensor("x", [TC, D], F32, kind="ExternalInput")
    wq_d = nc.dram_tensor("wq", [D, D], F32, kind="ExternalInput")
    wk_d = nc.dram_tensor("wk", [D, D], F32, kind="ExternalInput")
    wv_d = nc.dram_tensor("wv", [D, D], F32, kind="ExternalInput")
    wo_d = nc.dram_tensor("wo", [D, D], F32, kind="ExternalInput")
    b_d = [nc.dram_tensor(n, [D], F32, kind="ExternalInput")
           for n in ("bq", "bk", "bv", "bo")]
    gup_d = nc.dram_tensor("gup", [OSH, D // 8], I32, kind="ExternalInput")
    gus_d = nc.dram_tensor("gus", [NSG, OSH], F16, kind="ExternalInput")
    dnp_d = nc.dram_tensor("dnp", [DSH, IDIM // 8], I32, kind="ExternalInput")
    dns_d = nc.dram_tensor("dns", [NSD, DSH], F16, kind="ExternalInput")
    out_d = nc.dram_tensor("out", [TC, D], F32, kind="ExternalOutput")

    cc_k_in = nc.dram_tensor("cc_k_in", [D, TC], BF16)
    cc_kg = [nc.dram_tensor(f"cc_kg{i}", [NCORES * (D // 2), TC], BF16)
             for i in range(2)]
    cc_v_in = nc.dram_tensor("cc_v_in", [2 * TC, D // 2], BF16)
    cc_vg = [nc.dram_tensor(f"cc_vg{i}", [S, D // 2], BF16) for i in range(2)]
    cc_gu_in = nc.dram_tensor("cc_gu_in", [D, OSH], F16)
    cc_gug = nc.dram_tensor("cc_gug", [NCORES * D, OSH], F16)
    cc_dn_in = nc.dram_tensor("cc_dn_in", [IDIM, DSH], F16)
    cc_dng = nc.dram_tensor("cc_dng", [NCORES * IDIM, DSH], F16)
    x1_d = nc.dram_tensor("x1spill", [TC, D], F32)
    rg = [list(range(NCORES))]

    es = ExitStack()
    sPE = Sem(es, nc, "sPE", lazy=True)
    sACT = Sem(es, nc, "sACT", lazy=True)
    sDVE = Sem(es, nc, "sDVE", lazy=True)
    sPOOL = Sem(es, nc, "sPOOL", lazy=True)
    cK = [Sem(es, nc, f"cK{i}") for i in range(2)]
    cV = [Sem(es, nc, f"cV{i}") for i in range(2)]
    cGU = Sem(es, nc, "cGU")
    cDN = Sem(es, nc, "cDN")
    sPK = Sem(es, nc, "sPK")
    sSC = Sem(es, nc, "sSC")
    sX = [Sem(es, nc, f"sX{i}") for i in range(2)]
    sVT = [Sem(es, nc, f"sVT{i}") for i in range(6)]
    sK = [Sem(es, nc, f"sK{i}") for i in range(2)]
    sV = [Sem(es, nc, f"sV{i}") for i in range(2)]
    sCT = [Sem(es, nc, f"sCT{i}") for i in range(2)]
    sGS = [Sem(es, nc, f"sGS{i}") for i in range(2)]
    sWD = [Sem(es, nc, f"sWD{i}") for i in range(2)]
    sXR = [Sem(es, nc, f"sXR{i}") for i in range(3)]
    sOUT = [Sem(es, nc, f"sOUT{i}") for i in range(2)]
    gW = [Sem(es, nc, f"gW{i}") for i in range(3)]
    gB = Sem(es, nc, "gB")
    gK = [Sem(es, nc, f"gK{i}") for i in range(6)]
    gV = [Sem(es, nc, f"gV{i}") for i in range(6)]
    gDQ = [Sem(es, nc, f"gDQ{i}") for i in range(2)]
    gWO = [Sem(es, nc, f"gWO{i}") for i in range(4)]
    gSP = Sem(es, nc, "gSP")

    stA = es.enter_context(nc.psum_tensor("stA", [128, 1536], F32))
    stB = es.enter_context(nc.psum_tensor("stB", [128, 1536], F32))
    ctxAB = es.enter_context(nc.psum_tensor("ctxAB", [128, 1024], F32))
    stA_bf = stA.bitcast(BF16)
    stB_bf = stB.bitcast(BF16)
    ctx_bf = ctxAB.bitcast(BF16)
    PS8 = [stA[:, 0:512], stA[:, 512:1024], stA[:, 1024:1536],
           stB[:, 0:512], stB[:, 512:1024], stB[:, 1024:1536],
           ctxAB[:, 0:512], ctxAB[:, 512:1024]]
    TRREG = [stA_bf[:, 0:128], stA_bf[:, 1024:1152], stA_bf[:, 2048:2176],
             stB_bf[:, 0:128], stB_bf[:, 1024:1152], stB_bf[:, 2048:2176]]

    id_sb = es.enter_context(nc.sbuf_tensor("id_sb", [128, 128], BF16))
    bias_cols = es.enter_context(nc.sbuf_tensor("bias_cols", [128, 64], F32))
    c10_col = es.enter_context(nc.sbuf_tensor("c10_col", [128, 1], F32))
    eps5_col = es.enter_context(nc.sbuf_tensor("eps5_col", [128, 1], F32))
    eps6_col = es.enter_context(nc.sbuf_tensor("eps6_col", [128, 1], F32))
    ones1 = es.enter_context(nc.sbuf_tensor("ones1", [1, 1], F32))
    onesr = es.enter_context(nc.sbuf_tensor("onesr", [1, 128], F32))

    P = Plan()

    with nc.Block() as block:
        es_deq = ExitStack()
        gupT = [es_deq.enter_context(nc.sbuf_tensor(f"gupT{b}", [128, OSH], U16,
                                                    is_transpose_dest=True))
                for b in range(4)]
        dnpT = [es_deq.enter_context(nc.sbuf_tensor(f"dnpT{b}", [128, DSH], U16,
                                                    is_transpose_dest=True))
                for b in range(DNW_U)]
        scT_gu = es_deq.enter_context(nc.sbuf_tensor("scT_gu", [128, NB_D * OSH],
                                                     F16))
        scT_dn = es_deq.enter_context(nc.sbuf_tensor("scT_dn", [128, NB_I * DSH],
                                                     F16))
        DQW = max(OSH // 2, DSH)
        dq_scr = [es_deq.enter_context(nc.sbuf_tensor(f"dqs{i}", [128, DQW], U16))
                  for i in range(4)]
        dq_out = [es_deq.enter_context(nc.sbuf_tensor(f"dqo{i}", [128, DQW], F16))
                  for i in range(2)]

        es_q = ExitStack()
        q_fm = es_q.enter_context(nc.sbuf_tensor("q_fm", [128, NH * TC], BF16))

        es_sc = ExitStack()
        gus_sb = es_sc.enter_context(nc.sbuf_tensor("gus_sb", [NSG, OSH], F16))
        dns_sb = es_sc.enter_context(nc.sbuf_tensor("dns_sb", [NSD, DSH], F16))
        ind_gu = es_sc.enter_context(nc.sbuf_tensor("ind_gu", [NSG, NB_D * 128],
                                                    F16))
        ind_dn = es_sc.enter_context(nc.sbuf_tensor("ind_dn", [NSD, NB_I * 128],
                                                    F16))

        es_p1 = ExitStack()
        xbuf = [es_p1.enter_context(nc.sbuf_tensor(f"xbuf{i}", [128, D], F32))
                for i in range(2)]
        xl_tok = [es_p1.enter_context(nc.sbuf_tensor(f"xlt{i}", [128, D], BF16))
                  for i in range(2)]
        xl_fm = es_p1.enter_context(nc.sbuf_tensor("xl_fm", [128, 16 * TC], BF16))
        wsl = [es_p1.enter_context(nc.sbuf_tensor(f"wsl{i}", [128, 16 * 512],
                                                  BF16)) for i in range(3)]
        kev = [es_p1.enter_context(nc.sbuf_tensor(f"kev{i}", [128, 512], BF16))
               for i in range(6)]
        vpc = [es_p1.enter_context(nc.sbuf_tensor(f"vpc{i}", [128, 128], BF16,
                                                  is_transpose_dest=True))
               for i in range(6)]
        sums = es_p1.enter_context(nc.sbuf_tensor("sums", [128, NT], F32))
        ssq = es_p1.enter_context(nc.sbuf_tensor("ssq", [128, NT], F32))
        mcol = es_p1.enter_context(nc.sbuf_tensor("mcol", [128, NT], F32))
        rstd = es_p1.enter_context(nc.sbuf_tensor("rstd", [128, NT], F32))
        sq_scr = es_p1.enter_context(nc.sbuf_tensor("sq_scr", [128, D], F16))
        brow = es_p1.enter_context(nc.sbuf_tensor("brow", [1, 512], F32))

        idz = P.op("gpsimd", lambda e: e.memset(id_sb[:, :], 0.0), tag=sPOOL)
        P.op("gpsimd", lambda e: e.memset(c10_col[:, :], -CSHIFT), tag=sPOOL)
        P.op("gpsimd", lambda e: e.memset(eps5_col[:, :], 1e-5), tag=sPOOL)
        P.op("gpsimd", lambda e: e.memset(eps6_col[:, :], 1e-6 / 64.0), tag=sPOOL)
        P.op("gpsimd", lambda e: e.memset(ones1[:, :], 1.0), tag=sPOOL)
        P.op("gpsimd", lambda e: e.memset(onesr[:, :], 1.0), tag=sPOOL)
        id_done = P.op("gpsimd", lambda e: e.affine_select(
            out=id_sb[:, :], in_=id_sb[:, :], compare_op=ALU.not_equal, fill=1.0,
            base=0, pattern=[[-1, 128]], channel_multiplier=1),
            waits=[idz], tag=sPOOL)
        P.op("gpsimd", lambda e: e.memset(ind_gu[:, :], 0.0), tag=sPOOL)
        indz = P.op("gpsimd", lambda e: e.memset(ind_dn[:, :], 0.0), tag=sPOOL)
        ind_done = None
        for nb, ind in ((NB_D, ind_gu), (NB_I, ind_dn)):
            v3 = ind[:, :].rearrange("s (g w) -> s g w", w=32)
            ind_done = P.op("gpsimd", lambda e, v3=v3, nb=nb: e.affine_select(
                out=v3, in_=v3, compare_op=ALU.not_equal, fill=1.0, base=0,
                pattern=[[-1, nb * 4], [0, 32]], channel_multiplier=1),
                waits=[indz], tag=sPOOL)

        P.op("sync", lambda e: e.dma_start(out=gus_sb[:, :], in_=gus_d.ap()),
             tag=sSC, inc=16)
        scales_done = P.op("sync", lambda e: e.dma_start(out=dns_sb[:, :],
                                                         in_=dns_d.ap()),
                           tag=sSC, inc=16)
        gup_u = gup_d.bitcast(U16)
        packed_done = None
        for b in range(4):
            packed_done = P.op("sync", lambda e, b=b: e.dma_start_transpose(
                gupT[b][:, :], gup_u.ap()[:, b * 128:(b + 1) * 128]),
                tag=sPK, inc=16)
        dnp_u = dnp_d.bitcast(U16)
        for b in range(DNW_U):
            packed_done = P.op("sync", lambda e, b=b: e.dma_start_transpose(
                dnpT[b][:, :], dnp_u.ap()[:, b * 128:(b + 1) * 128]),
                tag=sPK, inc=16)

        x_done = [Z] * NT
        ln_apply = [Z] * NT
        ln_sq = [Z] * NT
        xl_tr_last = [Z] * NT
        xl_evac = [Z] * (NT * 16)

        def plan_xload(t):
            w = []
            if t >= 2:
                w = [ln_apply[t - 2], ln_sq[t - 2]]
            x_done[t] = P.op("sync", lambda e, t=t: e.dma_start(
                out=xbuf[t % 2][:, :], in_=x_d.ap()[t * 128:(t + 1) * 128, :]),
                waits=w, tag=sX[t % 2], inc=16)

        plan_xload(0)
        if NT > 1:
            plan_xload(1)
        for t in range(NT):
            xb = xbuf[t % 2]
            P.op("vector", lambda e, xb=xb, t=t: e.reduce_sum(
                sums[:, t:t + 1], xb[:, :], axis=AX), waits=[x_done[t]], tag=sDVE)
            P.op("vector", lambda e, t=t: e.tensor_scalar(
                mcol[:, t:t + 1], sums[:, t:t + 1], 1.0 / D, None, ALU.mult),
                waits=[sDVE.ms()], tag=sDVE)
            ln_sq[t] = P.op("scalar", lambda e, xb=xb, t=t: e.activation(
                sq_scr[:, :], xb[:, :], AF.Square, accum_out=ssq[:, t:t + 1]),
                waits=[x_done[t], sACT.ms()], tag=sACT)
            m2d = P.op("vector", lambda e, t=t: e.tensor_scalar(
                sums[:, t:t + 1], mcol[:, t:t + 1], mcol[:, t:t + 1], None,
                ALU.mult), waits=[sDVE.ms()], tag=sDVE)
            vard = P.op("vector", lambda e, t=t: e.scalar_tensor_tensor(
                out=rstd[:, t:t + 1], in0=ssq[:, t:t + 1], scalar=1.0 / D,
                in1=sums[:, t:t + 1], op0=ALU.mult, op1=ALU.subtract),
                waits=[m2d, ln_sq[t], sDVE.ms()], tag=sDVE)
            P.op("scalar", lambda e, t=t: e.activation(
                rstd[:, t:t + 1], rstd[:, t:t + 1], AF.Ln, bias=eps5_col[:, :]),
                waits=[vard, sACT.ms(), id_done], tag=sACT)
            rstd_done = P.op("scalar", lambda e, t=t: e.activation(
                rstd[:, t:t + 1], rstd[:, t:t + 1], AF.Exp, scale=-0.5),
                waits=[sACT.ms()], tag=sACT)
            w = [rstd_done, sDVE.ms()]
            if t >= 2:
                w.append(xl_tr_last[t - 2])
            ln_apply[t] = P.op("vector", lambda e, xb=xb, t=t: e.tensor_scalar(
                xl_tok[t % 2][:, :], xb[:, :], mcol[:, t:t + 1], rstd[:, t:t + 1],
                ALU.subtract, ALU.mult), waits=w, tag=sDVE)
            for dc in range(16):
                i = t * 16 + dc
                r = i % 6
                w = [ln_apply[t]]
                if i >= 6:
                    w.append(xl_evac[i - 6])
                xl_tr_last[t] = P.op("tensor", lambda e, t=t, dc=dc, r=r:
                                     e.transpose(
                                         TRREG[r][:, :],
                                         xl_tok[t % 2][:, dc * 128:(dc + 1) * 128],
                                         id_sb[:, :]), waits=w, tag=sPE)
                xl_evac[i] = P.op("vector", lambda e, t=t, dc=dc, r=r:
                                  e.tensor_copy(
                                      xl_fm[:, dc * TC + t * 128:
                                            dc * TC + (t + 1) * 128],
                                      TRREG[r][:, :]), waits=[sPE.ms()], tag=sDVE)
            if t + 2 < NT:
                plan_xload(t + 2)
        xl_all = xl_evac[-1]

        def w_cast(e, dst, src, q4):
            return e.dma_start(
                out=dst[:, :].rearrange("p (c o) -> p c o", c=16),
                in_=src.ap()[:, q4 * 512:(q4 + 1) * 512]
                .rearrange("(c p) o -> p c o", p=128))

        k_dma = [Z] * 32
        v_tr = [Z] * 128
        v_dma = [Z] * 128
        v_unit_last = [[] for _ in range(32)]
        v_pi = [0]
        grp_evac = {}
        grp_mm = {}
        slab_ready = {}

        def plan_slab(Q):
            pi, q4 = Q // 4, Q % 4
            wslab = wsl[Q % 3]
            wsrc = [wk_d, wv_d, wq_d][pi]
            w = []
            if Q >= 3:
                gprev = (Q - 3) * 2 + 1
                w.append(grp_mm[(gprev // 8, gprev % 8)])
            slab_ready[Q] = P.op("gpsimd", lambda e, wslab=wslab, wsrc=wsrc,
                                 q4=q4: w_cast(e, wslab, wsrc, q4),
                                 waits=w, tag=gW[Q % 3], inc=16)

        plan_slab(0)
        plan_slab(1)
        plan_slab(2)
        bias_last = Z
        for i in range(4):
            for qq in range(4):
                w = [bias_last] if (i or qq) else []
                brow_done = P.op("sync", lambda e, i=i, qq=qq: e.dma_start(
                    out=brow[:, :],
                    in_=b_d[i].ap()[qq * 512:(qq + 1) * 512][None, :]),
                    waits=w, tag=gB, inc=16)
                for ol in range(4):
                    oc = qq * 4 + ol
                    bias_last = P.op("tensor", lambda e, i=i, oc=oc, ol=ol:
                                     e.matmul(
                                         ctxAB[:, i * 16 + oc:i * 16 + oc + 1],
                                         brow[:, ol * 128:(ol + 1) * 128],
                                         ones1[:, :], start=True, stop=True),
                                     waits=([brow_done, id_done] if ol == 0
                                            else []), tag=sPE)
        bias_evac = P.op("vector", lambda e: e.tensor_copy(
            bias_cols[:, :], ctxAB[:, 0:64]), waits=[bias_last], tag=sDVE)

        PROJ_ORDER = [1, 2, 0]
        cc_issued = [False, False]
        def issue_half(kind, half):
            if kind == "K":
                ins_ = cc_k_in.ap()[half * (D // 2):(half + 1) * (D // 2), :]
                outs_ = cc_kg[half].ap()
                sem, dep = cK[half], gK
            else:
                ins_ = cc_v_in.ap()[half * TC:(half + 1) * TC, :]
                outs_ = cc_vg[half].ap()
                sem, dep = cV[half], gV
            P.op("gpsimd", lambda e, ins_=ins_, outs_=outs_: e.collective_compute(
                "AllGather", ALU.bypass, replica_groups=rg,
                ins=[ins_.opt()], outs=[outs_.opt()]),
                waits=[s.ms() for s in dep], tag=sem)

        for pi in range(3):
            proj = PROJ_ORDER[pi]
            for g in range(8):
                if g == 4:
                    if pi == 1:
                        issue_half("K", 0)
                    elif pi == 2:
                        issue_half("V", 0)
                Q = pi * 4 + g // 2
                wslab = wsl[Q % 3]
                par = g % 2
                regs = PS8[par * 4:par * 4 + 4]
                for dc in range(16):
                    for oc2 in range(2):
                        oc = g * 2 + oc2
                        ol = oc % 4
                        for th in range(NTH):
                            w = []
                            if dc == 0 and oc2 == 0 and th == 0:
                                w = [xl_all, slab_ready[Q], bias_evac]
                                prev = (pi * 8 + g) - 2
                                if prev >= 0:
                                    w.append(grp_evac[(prev // 8, prev % 8)])
                            grp_mm[(pi, g)] = P.op(
                                "tensor",
                                lambda e, wslab=wslab, dc=dc, ol=ol,
                                reg=regs[oc2 * 2 + (th % 2)], th=th: e.matmul(
                                    reg[:, 0:TH],
                                    wslab[:, dc * 512 + ol * 128:
                                          dc * 512 + (ol + 1) * 128],
                                    xl_fm[:, dc * TC + th * TH:
                                          dc * TC + (th + 1) * TH],
                                    start=(dc == 0), stop=(dc == 15)),
                                waits=w, tag=sPE)
                for oc2 in range(2):
                    oc = g * 2 + oc2
                    for th in range(NTH):
                        reg = regs[oc2 * 2 + (th % 2)]
                        bcol = bias_cols[:, proj * 16 + oc:proj * 16 + oc + 1]
                        u = (g * 2 + oc2) * NTH + th
                        if proj == 0:
                            dst = q_fm[:, oc * TC + th * TH:oc * TC + (th + 1) * TH]
                            grp_evac[(pi, g)] = P.op(
                                "vector", lambda e, dst=dst, reg=reg, bcol=bcol:
                                e.tensor_scalar(dst, reg[:, 0:TH], bcol, None,
                                                ALU.add),
                                waits=[grp_mm[(pi, g)], sDVE.ms()], tag=sDVE)
                        elif proj == 1:
                            kv = kev[u % 6]
                            w = [grp_mm[(pi, g)], sDVE.ms()]
                            if u >= 6:
                                w.append(k_dma[u - 6])
                            grp_evac[(pi, g)] = P.op(
                                "vector", lambda e, kv=kv, reg=reg, bcol=bcol:
                                e.tensor_scalar(kv[:, 0:TH], reg[:, 0:TH], bcol,
                                                None, ALU.add), waits=w, tag=sDVE)
                            k_dma[u] = P.op("gpsimd", lambda e, kv=kv, oc=oc,
                                            th=th: e.dma_start(
                                                out=cc_k_in.ap()[
                                                    oc * 128:(oc + 1) * 128,
                                                    th * TH:(th + 1) * TH],
                                                in_=kv[:, 0:TH]),
                                            waits=[grp_evac[(pi, g)]],
                                            tag=gK[u % 6], inc=16)
                        else:
                            vv = kev[(32 + u) % 6]
                            w = [grp_mm[(pi, g)], sDVE.ms()]
                            prevq = 32 + u - 6
                            if prevq >= 32:
                                w.extend(v_unit_last[prevq - 32])
                            else:
                                w.append(k_dma[prevq])
                            grp_evac[(pi, g)] = P.op(
                                "vector", lambda e, vv=vv, reg=reg, bcol=bcol:
                                e.tensor_scalar(vv[:, 0:TH], reg[:, 0:TH], bcol,
                                                None, ALU.add), waits=w, tag=sDVE)
                            for sub in range(TH // 128):
                                pi2 = v_pi[0]
                                v_pi[0] += 1
                                pc = vpc[pi2 % 6]
                                w2 = [grp_evac[(pi, g)]]
                                if pi2 >= 6:
                                    w2.append(v_dma[pi2 - 6])
                                v_tr[pi2] = P.op("sync",
                                                 lambda e, pc=pc, vv=vv, sub=sub:
                                                 e.dma_start_transpose(
                                                     pc[:, :],
                                                     vv[:, sub * 128:
                                                        (sub + 1) * 128]),
                                                 waits=w2, tag=sVT[pi2 % 6],
                                                 inc=16)
                                trow = th * TH + sub * 128
                                vrow = (oc // 8) * TC + trow
                                vcol = (oc % 8) * 128
                                v_dma[pi2] = P.op("gpsimd",
                                                  lambda e, pc=pc, vrow=vrow,
                                                  vcol=vcol: e.dma_start(
                                                      out=cc_v_in.ap()[
                                                          vrow:vrow + 128,
                                                          vcol:vcol + 128],
                                                      in_=pc[:, :]),
                                                  waits=[v_tr[pi2]],
                                                  tag=gV[pi2 % 6], inc=16)
                                v_unit_last[u].append(v_tr[pi2])
                if g % 2 == 1 and Q + 3 < 12:
                    plan_slab(Q + 3)
            if pi == 1:
                issue_half("K", 1)
            elif pi == 2:
                issue_half("V", 1)
        qkv_evac_all = grp_evac[(2, 7)]

        reg_i = 0
        sc_evac = []
        for kind in range(2):
            nb = NB_D if kind == 0 else NB_I
            wdt_tot = OSH if kind == 0 else DSH
            ind = ind_gu if kind == 0 else ind_dn
            scf = gus_sb if kind == 0 else dns_sb
            sct = scT_gu if kind == 0 else scT_dn
            for b in range(nb):
                for o0 in range(0, wdt_tot, 512):
                    wdt = min(512, wdt_tot - o0)
                    reg = PS8[reg_i % 4]
                    w = [scales_done, ind_done, qkv_evac_all]
                    if reg_i >= 4:
                        w.append(sc_evac[reg_i - 4])
                    mm = P.op("tensor", lambda e, b=b, o0=o0, wdt=wdt, reg=reg,
                              ind=ind, scf=scf: e.matmul(
                                  reg[:, 0:wdt], ind[:, b * 128:(b + 1) * 128],
                                  scf[:, o0:o0 + wdt], start=True, stop=True),
                              waits=w, tag=sPE)
                    sc_evac.append(P.op("vector", lambda e, b=b, o0=o0, wdt=wdt,
                                        reg=reg, sct=sct, wt=wdt_tot:
                                        e.tensor_copy(
                                            sct[:, b * wt + o0:
                                                b * wt + o0 + wdt],
                                            reg[:, 0:wdt]),
                                        waits=[mm, sDVE.ms()], tag=sDVE))
                    reg_i += 1
        scT_done = sc_evac[-1]

        es_bo = ExitStack()
        bo_bcast = es_bo.enter_context(nc.sbuf_tensor("bo_bcast", [128, D], F32,
                                                      side="right"))
        bo_evac = Z
        bo_mm = bias_last
        for ds_ in range(4):
            reg = PS8[4 + ds_ % 2]
            bo_row_done = P.op("sync", lambda e, ds_=ds_: e.dma_start(
                out=brow[:, :],
                in_=b_d[3].ap()[ds_ * 512:(ds_ + 1) * 512][None, :]),
                waits=[bo_mm], tag=gB, inc=16)
            w = [bo_row_done, qkv_evac_all]
            if ds_ >= 2:
                w.append(bo_evac)
            bo_mm = P.op("tensor", lambda e, reg=reg: e.matmul(
                reg[:, 0:512], onesr[:, :], brow[:, 0:512],
                start=True, stop=True), waits=w, tag=sPE)
            bo_evac = P.op("vector", lambda e, ds_=ds_, reg=reg: e.tensor_copy(
                bo_bcast[:, ds_ * 512:(ds_ + 1) * 512], reg[:, 0:512]),
                waits=[bo_mm, sDVE.ms()], tag=sDVE)

        es_p1.close()
        es_sc.close()

        es_ctx = ExitStack()
        ctx_fm = es_ctx.enter_context(nc.sbuf_tensor("ctx_fm", [128, NH * TC],
                                                     BF16, is_transpose_dest=True,
                                                     side="right"))
        es_wo0 = ExitStack()
        wo_cb0 = es_wo0.enter_context(nc.sbuf_tensor("wocb0", [128, 16 * 512],
                                                     BF16, side="right"))
        es_att = ExitStack()
        K_buf = [es_att.enter_context(nc.sbuf_tensor(f"Kb{i}", [128, S], BF16))
                 for i in range(2)]
        V_buf = [es_att.enter_context(nc.sbuf_tensor(f"Vb{i}", [128, KT * 129],
                                                     BF16)) for i in range(2)]
        Pslab = [es_att.enter_context(nc.sbuf_tensor(f"Ps{i}", [128, 3 * QTW],
                                                     BF16)) for i in range(2)]
        ctx_n = [es_att.enter_context(nc.sbuf_tensor(f"cn{i}", [128, 128], BF16))
                 for i in range(2 * NQS)]
        rcp = es_att.enter_context(nc.sbuf_tensor("rcp", [128, 2 * NQS], F32))

        ph2_pre = [sPE.ms(), sDVE.ms(), sACT.ms()]
        vone = [Z, Z]
        for i in range(2):
            vone[i] = P.op("gpsimd", lambda e, i=i: e.memset(
                V_buf[i][:, :].rearrange("p (k e) -> p k e", e=129)[:, :, 128:129],
                1.0), waits=ph2_pre, tag=sPOOL)
        wo0_ready = P.op("gpsimd", lambda e: e.dma_start(
            out=wo_cb0[:, :].rearrange("p (c o) -> p c o", c=16),
            in_=wo_d.ap()[:, 0:512].rearrange("(c p) o -> p c o", p=128)),
            waits=ph2_pre, tag=gWO[0], inc=16)

        kap = [cc_kg[i].ap().rearrange("(r hh p) t -> r hh p t", hh=NH // 2,
                                       p=128) for i in range(2)]
        vap = [cc_vg[i].ap().rearrange("(kt p) (hh dd) -> kt p hh dd", p=128,
                                       dd=128) for i in range(2)]
        kv_load = [None] * NH

        def plan_kv_load(h, after_pe):
            buf = h % 2
            hf, hl = h // (NH // 2), h % (NH // 2)
            w = [(cK[hf], 1)]
            if after_pe != Z:
                w.append(after_pe)
            else:
                w += ph2_pre
            kc = P.op("sync", lambda e, hf=hf, hl=hl, buf=buf: e.dma_start(
                out=K_buf[buf][:, :].rearrange("p (r t) -> p r t", r=NCORES),
                in_=kap[hf][:, hl, :, :].transpose([1, 0, 2])),
                waits=w, tag=sK[buf], inc=16)
            vc = Z
            for sp in range(VSPL):
                kt0 = sp * KT_PER_V
                kt1 = min(KT, kt0 + KT_PER_V)
                w2 = []
                if sp == 0:
                    w2 = [(cV[hf], 1), vone[buf]]
                    if after_pe != Z:
                        w2.append(after_pe)
                    else:
                        w2 += ph2_pre
                vc = P.op("sync", lambda e, hf=hf, hl=hl, buf=buf, kt0=kt0,
                          kt1=kt1: e.dma_start(
                              out=V_buf[buf][:, :].rearrange(
                                  "p (k e) -> p k e", e=129)[:, kt0:kt1, 0:128],
                              in_=vap[hf][:, :, hl, :].transpose([1, 0, 2])
                              [:, kt0:kt1, :]),
                          waits=w2, tag=sV[buf], inc=16)
            kv_load[h] = (kc, vc)

        plan_kv_load(0, Z)
        if NH > 1:
            plan_kv_load(1, Z)

        dq_chunks = ([("gu", b, g, hf) for b in range(4) for g in range(4)
                      for hf in range(2)] +
                     [("dn", b, g, 0) for b in range(NB_I) for g in range(4)])
        n_dq = len(dq_chunks)
        dq_dma = [Z] * n_dq

        def plan_dq_chunk(ci):
            kind, b, g, hf = dq_chunks[ci]
            if kind == "gu":
                wdt = OSH // 2
                src = gupT[b][:, hf * wdt:(hf + 1) * wdt]
                sct = scT_gu[:, b * OSH + hf * wdt:b * OSH + (hf + 1) * wdt]
                cdst, row, col0 = cc_gu_in, (4 * b + g) * 128, hf * wdt
            else:
                wdt = DSH
                src = dnpT[b][:, 0:wdt]
                sct = scT_dn[:, b * DSH:(b + 1) * DSH]
                cdst, row, col0 = cc_dn_in, (4 * b + g) * 128, 0
            em2, s2, t1, t2 = [t[:, 0:wdt] for t in dq_scr]
            raw = em2
            out = dq_out[ci % 2][:, 0:wdt]
            m_em, m_s = 7 << (4 * g), 8 << (4 * g)
            sh, shs = 9 - 4 * g, 12 - 4 * g
            w0 = [packed_done, scT_done, sDVE.ms()]
            if ci >= 2:
                w0.append(dq_dma[ci - 2])

            def step(fn, w=None):
                return P.op("vector", fn, waits=(w or [sDVE.ms()]), tag=sDVE)
            if sh >= 0:
                step(lambda e: e.tensor_scalar(em2, src, m_em, sh,
                                               ALU.bitwise_and,
                                               ALU.logical_shift_left), w0)
            else:
                step(lambda e: e.tensor_scalar(em2, src, m_em, -sh,
                                               ALU.bitwise_and,
                                               ALU.logical_shift_right), w0)
            step(lambda e: e.tensor_scalar(em2, em2, 0x3800, None, ALU.add))
            if shs > 0:
                step(lambda e: e.tensor_scalar(s2, src, m_s, shs,
                                               ALU.bitwise_and,
                                               ALU.logical_shift_left))
            else:
                step(lambda e: e.tensor_scalar(s2, src, m_s, None,
                                               ALU.bitwise_and))
            rawf = raw.bitcast(F16)
            t1f, t2f = t1.bitcast(F16), t2.bitcast(F16)
            step(lambda e: e.tensor_scalar(t1f, rawf, 2.0, -1.0, ALU.mult,
                                           ALU.add))
            step(lambda e: e.tensor_scalar(t2f, rawf, -0.5, None, ALU.add))
            step(lambda e: e.tensor_tensor(t1f, t1f, t2f, ALU.max))
            step(lambda e: e.tensor_tensor(t1f, rawf, t1f, ALU.min))
            if kind == "gu":
                K4 = wdt // 4
                last = Z
                for g4 in range(4):
                    step(lambda e, g4=g4: e.tensor_tensor(
                        out.bitcast(U16)[:, g4 * K4:(g4 + 1) * K4],
                        t1[:, g4::4], s2[:, g4::4], ALU.bitwise_or))
                    last = step(lambda e, g4=g4: e.tensor_tensor(
                        out[:, g4 * K4:(g4 + 1) * K4],
                        out[:, g4 * K4:(g4 + 1) * K4],
                        sct[:, g4::4], ALU.mult))
            else:
                step(lambda e: e.tensor_tensor(out.bitcast(U16), t1, s2,
                                               ALU.bitwise_or))
                last = step(lambda e: e.tensor_tensor(out, out, sct, ALU.mult))
            if kind == "gu":
                K4 = wdt // 4
                dst = cc_gu_in.ap()[row:row + 128, :].rearrange(
                    "r (g k) -> r g k", g=4)[:, :, (col0 // 4):(col0 // 4) + K4]
                dq_dma[ci] = P.op("gpsimd", lambda e, out=out, dst=dst, K4=K4:
                                  e.dma_start(
                                      out=dst,
                                      in_=out.rearrange("p (g k) -> p g k", g=4)),
                                  waits=[last], tag=gDQ[ci % 2], inc=16)
            else:
                dq_dma[ci] = P.op("gpsimd", lambda e, out=out, cdst=cdst, row=row,
                                  wdt=wdt, col0=col0: e.dma_start(
                                      out=cdst.ap()[row:row + 128,
                                                    col0:col0 + wdt],
                                      in_=out),
                                  waits=[last], tag=gDQ[ci % 2], inc=16)

        exp_all = {}
        pv_all = {}
        ev_done = [Z] * NITS
        tr_done = [Z] * NITS
        pe_head_end = [Z] * NH
        dq_i = 0

        def plan_it_prologue(it):
            nonlocal dq_i
            h, qt = it // QT, it % QT
            if qt == 0 and h >= 1 and h + 1 < NH:
                plan_kv_load(h + 1, pe_head_end[h - 1])
            for _ in range(3):
                if dq_i < n_dq:
                    plan_dq_chunk(dq_i)
                    dq_i += 1

        def plan_S_exp(G):
            it, k = G // NG, G % NG
            h, qt = it // QT, it % QT
            g0, gl = groups[k]
            stX = stA if G % 2 == 0 else stB
            w = [kv_load[h][0]]
            if G >= 2:
                w.append(exp_all[G - 2])
            else:
                w.append(scT_done)
                w.append(bo_evac)
            smm = None
            for j in range(gl):
                kt = g0 + j
                smm = P.op("tensor", lambda e, stX=stX, j=j, kt=kt, h=h,
                           qt=qt: e.matmul(
                               stX[:, j * 512:j * 512 + QTW],
                               K_buf[h % 2][:, kt * 128:(kt + 1) * 128],
                               q_fm[:, h * TC + qt * QTW:
                                    h * TC + (qt + 1) * QTW],
                               start=True, stop=True),
                           waits=(w if j == 0 else []), tag=sPE)
            wE = [smm]
            if G >= 2:
                wE.append(pv_all[G - 2])
            exp_all[G] = P.op("scalar", lambda e, stX=stX, gl=gl, G=G:
                              e.activation(
                                  Pslab[G % 2][:, 0:gl * QTW].rearrange(
                                      "p (j q) -> p j q", q=QTW),
                                  stX[:, 0:1536].rearrange(
                                      "p (j q) -> p j q", q=512)
                                  [:, 0:gl, 0:QTW],
                                  AF.Exp, bias=c10_col[:, :], scale=SCALE_QK),
                              waits=wE, tag=sACT)

        def plan_PV(G):
            it, k = G // NG, G % NG
            h, qt = it // QT, it % QT
            buf = h % 2
            g0, gl = groups[k]
            wP = [exp_all[G], kv_load[h][1]]
            if k == 0 and it > 0:
                wP.append(ev_done[it - 1])
            pmm = None
            for j in range(gl):
                kt = g0 + j
                for qs in range(NQS):
                    off = (qs // 2) * 512 + (qs % 2) * 136
                    st = (kt == 0 and qs % 2 == 0)
                    sp = (kt == KT - 1 and (qs % 2 == 1 or qs == NQS - 1))
                    pmm = P.op("tensor",
                               lambda e, G=G, j=j, kt=kt, qs=qs, off=off,
                               buf=buf, st=st, sp=sp: e.matmul(
                                   ctxAB[:, off:off + 129],
                                   Pslab[G % 2][:, j * QTW + qs * 128:
                                                j * QTW + (qs + 1) * 128],
                                   V_buf[buf][:, kt * 129:(kt + 1) * 129],
                                   start=st, stop=sp),
                               waits=(wP if (j == 0 and qs == 0) else []),
                               tag=sPE)
            pv_all[G] = pmm
            if k == NG - 1:
                pe_head_end[h] = pmm

        def plan_evac(it):
            h, qt = it // QT, it % QT
            for qs in range(NQS):
                off = (qs // 2) * 512 + (qs % 2) * 136
                ci = (it % 2) * NQS + qs
                w = [pv_all[it * NG + NG - 1], sDVE.ms()]
                if it >= 2:
                    w.append(tr_done[it - 2])
                P.op("vector", lambda e, off=off, ci=ci: e.reciprocal(
                    rcp[:, ci:ci + 1], ctxAB[:, off + 128:off + 129]),
                    waits=w, tag=sDVE)
                ev_done[it] = P.op("vector", lambda e, off=off, ci=ci:
                                   e.tensor_scalar(
                                       ctx_n[ci][:, :], ctxAB[:, off:off + 128],
                                       rcp[:, ci:ci + 1], None, ALU.mult),
                                   waits=[sDVE.ms()], tag=sDVE)
            for qs in range(NQS):
                ci = (it % 2) * NQS + qs
                col = h * TC + qt * QTW + qs * 128
                tr_done[it] = P.op("sync", lambda e, ci=ci, col=col:
                                   e.dma_start_transpose(
                                       ctx_fm[:, col:col + 128], ctx_n[ci][:, :]),
                                   waits=[ev_done[it]], tag=sCT[it % 2], inc=16)

        TOT_G = NITS * NG
        for G in range(TOT_G):
            it, k = G // NG, G % NG
            if k == 0:
                if G >= 1:
                    plan_PV(G - 1)
                    plan_evac((G - 1) // NG)
                plan_it_prologue(it)
                plan_S_exp(G)
            else:
                plan_S_exp(G)
                plan_PV(G - 1)
        plan_PV(TOT_G - 1)
        plan_evac(NITS - 1)
        while dq_i < n_dq:
            plan_dq_chunk(dq_i)
            dq_i += 1

        P.op("gpsimd", lambda e: e.collective_compute(
            "AllGather", ALU.bypass, replica_groups=rg,
            ins=[cc_gu_in.ap().opt()], outs=[cc_gug.ap().opt()]),
            waits=[gDQ[0].ms(), gDQ[1].ms()], tag=cGU)
        P.op("gpsimd", lambda e: e.collective_compute(
            "AllGather", ALU.bypass, replica_groups=rg,
            ins=[cc_dn_in.ap().opt()], outs=[cc_dng.ap().opt()]),
            waits=[], tag=cDN)

        es_att.close()
        es_q.close()
        es_deq.close()

        es_ffn0 = ExitStack()
        gsl0 = es_ffn0.enter_context(nc.sbuf_tensor("gsl0", [128, 16 * 512],
                                                    F16))
        es_xl2 = ExitStack()
        xl2_fm = es_xl2.enter_context(nc.sbuf_tensor("xl2_fm", [128, 16 * TC],
                                                     BF16))
        es_p3 = ExitStack()
        wo_cbr = [es_p3.enter_context(nc.sbuf_tensor(f"wocb{i}", [128, 16 * 512],
                                                     BF16)) for i in range(1, 4)]
        wo_cb = [wo_cb0] + wo_cbr
        x1t3 = [es_p3.enter_context(nc.sbuf_tensor(f"x1p{i}", [128, D], F32))
                for i in range(2)]
        xbuf2 = [es_p3.enter_context(nc.sbuf_tensor(f"xb2{i}", [128, D], F32))
                 for i in range(2)]
        xl2_tok = [es_p3.enter_context(nc.sbuf_tensor(f"xl2t{i}", [128, D], BF16))
                   for i in range(2)]
        sums2 = es_p3.enter_context(nc.sbuf_tensor("sums2", [128, NT], F32))
        ssq2 = es_p3.enter_context(nc.sbuf_tensor("ssq2", [128, NT], F32))
        mcol2 = es_p3.enter_context(nc.sbuf_tensor("mcol2", [128, NT], F32))
        rstd2 = es_p3.enter_context(nc.sbuf_tensor("rstd2", [128, NT], F32))
        sq_scr2 = es_p3.enter_context(nc.sbuf_tensor("sq_scr2", [128, D], F16))

        TR4 = {0: [stA_bf[:, 0:128], stA_bf[:, 1024:1152],
                   stA_bf[:, 2048:2176], stB_bf[:, 0:128]],
               1: [stB_bf[:, 1024:1152], stB_bf[:, 2048:2176],
                   ctx_bf[:, 0:128], ctx_bf[:, 1024:1152]]}

        att_done = [sPE.ms(), sDVE.ms(), sACT.ms(), sCT[0].ms(), sCT[1].ms(),
                    gDQ[0].ms(), gDQ[1].ms()]

        gap = cc_gug.ap().rearrange("(r c p) o -> r c p o", c=16, p=128)

        def plan_gu_slab(dst, oglob0, extra, sem):
            first = True
            for r in range(NCORES):
                lo = max(oglob0, r * OSH)
                hi = min(oglob0 + 512, (r + 1) * OSH)
                if lo >= hi:
                    continue
                w = [(cGU, 1)] + (list(extra) if first else [])
                first = False
                l4, h4 = (lo - r * OSH) // 4, (hi - r * OSH) // 4
                d4 = (lo - oglob0) // 4
                gv = gap[r, :, :, :].rearrange("c p (g k) -> c p g k", g=4)
                for g4 in range(4):
                    P.op("sync", lambda e, dst=dst, gv=gv, l4=l4, h4=h4, d4=d4,
                         g4=g4: e.dma_start(
                             out=dst[:, :].rearrange("p (c g k) -> p c g k",
                                                     c=16, g=4)
                             [:, :, g4, d4:d4 + (h4 - l4)],
                             in_=gv[:, :, g4, l4:h4].transpose([1, 0, 2])),
                         waits=w, tag=sem, inc=16)
                    w = []

        plan_gu_slab(gsl0, 0, list(att_done), sGS[0])
        wo_ready = {0: wo0_ready}
        for dch in range(1, 4):
            wo_ready[dch] = P.op("gpsimd", lambda e, dch=dch: e.dma_start(
                out=wo_cb[dch][:, :].rearrange("p (c o) -> p c o", c=16),
                in_=wo_d.ap()[:, dch * 512:(dch + 1) * 512]
                .rearrange("(c p) o -> p c o", p=128)),
                waits=list(att_done), tag=gWO[dch], inc=16)

        op_mm = {}
        op_evac = {}
        x2_done = [Z] * NT
        ln2_apply = [Z] * NT
        spill = [Z] * NT
        xc2_done = [Z] * NT
        xl2_evac = {}
        xl2_evac_last = [Z] * NT

        def plan_tr(tt):
            par = tt % 2
            for c in range(16):
                r = c % 4
                w2 = [ln2_apply[tt], op_evac[(tt, 3)]]
                if c >= 4:
                    w2.append(xl2_evac[(tt, c - 4)])
                b, g = c // 4, c % 4
                P.op("tensor", lambda e, tt=tt, b=b, g=g, r=r, par=par:
                     e.transpose(TR4[par][r][:, :],
                                 xl2_tok[tt % 2][:, 512 * b + g:
                                                 512 * b + g + 509:4],
                                 id_sb[:, :]), waits=w2, tag=sPE)
                xl2_evac[(tt, c)] = P.op("scalar", lambda e, tt=tt, c=c, r=r,
                                         par=par: e.copy(
                                             xl2_fm[:, c * TC + tt * 128:
                                                    c * TC + (tt + 1) * 128],
                                             TR4[par][r][:, :]),
                                         waits=[sPE.ms(), sACT.ms()], tag=sACT)
            xl2_evac_last[tt] = xl2_evac[(tt, 15)]

        for t in range(NT):
            par = t % 2
            for dch in range(4):
                reg = PS8[par * 4 + dch]
                w = [wo_ready[dch]]
                if t == 0 and dch == 0:
                    w += att_done
                if (t - 2, dch) in op_evac:
                    w.append(op_evac[(t - 2, dch)])
                if t >= 2:
                    w.append(xl2_evac_last[t - 2] if xl2_evac_last[t - 2] != Z
                             else Z)
                for hh in range(NH):
                    op_mm[(dch, t)] = P.op(
                        "tensor", lambda e, t=t, hh=hh, dch=dch, reg=reg:
                        e.matmul(
                            reg[:, 0:512],
                            ctx_fm[:, hh * TC + t * 128:hh * TC + (t + 1) * 128],
                            wo_cb[dch][:, hh * 512:(hh + 1) * 512],
                            start=(hh == 0), stop=(hh == NH - 1)),
                        waits=(w if hh == 0 else []), tag=sPE)
                we = [op_mm[(dch, t)], sDVE.ms()]
                if dch == 0 and t >= 2:
                    we += [gSP.ms(), xc2_done[t - 2], ln2_apply[t - 2]]
                op_evac[(t, dch)] = P.op(
                    "vector", lambda e, dch=dch, t=t, reg=reg, par=par:
                    e.tensor_tensor(
                        x1t3[par][:, dch * 512:(dch + 1) * 512],
                        reg[:, 0:512], bo_bcast[:, dch * 512:(dch + 1) * 512],
                        ALU.add),
                    waits=we, tag=sDVE)
            w = list(att_done) if t < 2 else [ln2_apply[t - 2]]
            x2_done[t] = P.op("sync", lambda e, t=t, par=par: e.dma_start(
                out=xbuf2[par][:, :], in_=x_d.ap()[t * 128:(t + 1) * 128, :]),
                waits=w, tag=sX[par], inc=16)
            add_done = P.op("vector", lambda e, t=t, par=par: e.tensor_tensor(
                x1t3[par][:, :], x1t3[par][:, :], xbuf2[par][:, :], ALU.add),
                waits=[x2_done[t], op_evac[(t, 3)], sDVE.ms()], tag=sDVE)
            spill[t] = P.op("gpsimd", lambda e, t=t, par=par: e.dma_start(
                out=x1_d.ap()[t * 128:(t + 1) * 128, :], in_=x1t3[par][:, :]),
                waits=[add_done], tag=gSP, inc=16)
            P.op("vector", lambda e, t=t, par=par: e.reduce_sum(
                sums2[:, t:t + 1], x1t3[par][:, :], axis=AX),
                waits=[add_done], tag=sDVE)
            P.op("vector", lambda e, t=t: e.tensor_scalar(
                mcol2[:, t:t + 1], sums2[:, t:t + 1], 1.0 / D, None, ALU.mult),
                waits=[sDVE.ms()], tag=sDVE)
            sq2 = P.op("scalar", lambda e, t=t, par=par: e.activation(
                sq_scr2[:, :], x1t3[par][:, :], AF.Square,
                accum_out=ssq2[:, t:t + 1]),
                waits=[add_done, sACT.ms()], tag=sACT)
            xc2_done[t] = sq2
            m2d = P.op("vector", lambda e, t=t: e.tensor_scalar(
                sums2[:, t:t + 1], mcol2[:, t:t + 1], mcol2[:, t:t + 1], None,
                ALU.mult), waits=[sDVE.ms()], tag=sDVE)
            vard = P.op("vector", lambda e, t=t: e.scalar_tensor_tensor(
                out=rstd2[:, t:t + 1], in0=ssq2[:, t:t + 1], scalar=1.0 / D,
                in1=sums2[:, t:t + 1], op0=ALU.mult, op1=ALU.subtract),
                waits=[m2d, sq2, sDVE.ms()], tag=sDVE)
            P.op("scalar", lambda e, t=t: e.activation(
                rstd2[:, t:t + 1], rstd2[:, t:t + 1], AF.Ln, bias=eps5_col[:, :]),
                waits=[vard, sACT.ms()], tag=sACT)
            r2 = P.op("scalar", lambda e, t=t: e.activation(
                rstd2[:, t:t + 1], rstd2[:, t:t + 1], AF.Exp, scale=-0.5),
                waits=[sACT.ms()], tag=sACT)
            w = [r2, sDVE.ms()]
            if t >= 2:
                w.append(xl2_evac_last[t - 2])
            ln2_apply[t] = P.op("vector", lambda e, t=t, par=par: e.tensor_scalar(
                xl2_tok[par][:, :], x1t3[par][:, :], mcol2[:, t:t + 1],
                rstd2[:, t:t + 1], ALU.subtract, ALU.mult), waits=w, tag=sDVE)
            if t >= 1:
                plan_tr(t - 1)
        plan_tr(NT - 1)

        es_p3.close()
        es_wo0.close()
        es_ctx.close()
        es_bo.close()

        es_h = ExitStack()
        h_buf = es_h.enter_context(nc.sbuf_tensor("h_buf", [128, NCH_DN * TC],
                                                  F16, side="right"))
        es_p4 = ExitStack()
        gsl = [gsl0, es_p4.enter_context(nc.sbuf_tensor("gsl1", [128, 16 * 512],
                                                        F16))]
        usl = [es_p4.enter_context(nc.sbuf_tensor(f"usl{i}", [128, 16 * 512],
                                                  F16)) for i in range(2)]
        silu_sb = [es_p4.enter_context(nc.sbuf_tensor(f"silu{i}", [128, 1024],
                                                      F16)) for i in range(2)]

        ph3_done = [sPE.ms(), sDVE.ms(), sACT.ms(), gSP.ms(), sX[0].ms(),
                    sX[1].ms(), gWO[0].ms(), gWO[1].ms()]

        pair_mm_u = {}
        pair_mm_g = {}
        silu_done = {}
        hmul_done = {}
        pair_pe_end = {}
        TCC = min(1024, TC)
        for Bp in range(NB_I):
            bi = Bp % 2
            pe_free = pair_pe_end.get(Bp - 2, Z)
            if Bp == 0:
                plan_gu_slab(usl[0], IDIM, list(ph3_done), sGS[0])
                slab_done = sGS[0].ms()
            else:
                if Bp == 1:
                    extra = list(ph3_done)
                else:
                    extra = [pe_free] if pe_free != Z else []
                plan_gu_slab(gsl[bi], 512 * Bp, extra, sGS[bi])
                plan_gu_slab(usl[bi], IDIM + 512 * Bp, [], sGS[bi])
                slab_done = sGS[bi].ms()
            for g in range(4):
                p = Bp * 4 + g
                par = p % 2
                if par == 0:
                    rg_g = [PS8[0], PS8[1]]
                    rg_u = [PS8[3], PS8[4]]
                    gate_span = stA[:, 0:1024]
                    up_span = stB[:, 0:1024]
                else:
                    rg_g = [PS8[6], PS8[7]]
                    rg_u = [PS8[2], PS8[5]]
                    gate_span = ctxAB[:, 0:1024]
                    up_span = None
                for ui, (slab, regs) in enumerate(((gsl[bi], rg_g),
                                                   (usl[bi], rg_u))):
                    for th in range(NTH):
                        for c in range(16):
                            w = []
                            if th == 0 and c == 0:
                                w = [slab_done]
                                if ui == 0:
                                    w.append(silu_done.get(p - 2, Z))
                                    if p == 0:
                                        w += ph3_done
                                else:
                                    w.append(hmul_done.get(p - 2, Z))
                            mm = P.op("tensor", lambda e, slab=slab, c=c, g=g,
                                      th=th, reg=regs[th % 2]: e.matmul(
                                          reg[:, 0:TH],
                                          slab[:, :].rearrange(
                                              "p (cc g4 k) -> p cc g4 k",
                                              cc=16, g4=4)[:, c, g, :],
                                          xl2_fm[:, c * TC + th * TH:
                                                 c * TC + (th + 1) * TH],
                                          start=(c == 0), stop=(c == 15)),
                                      waits=w, tag=sPE)
                            if ui == 0:
                                pair_mm_g[p] = mm
                            else:
                                pair_mm_u[p] = mm
                silu_done[p] = P.op("scalar", lambda e, gate_span=gate_span, p=p:
                                    e.activation(silu_sb[p % 2][:, 0:TCC],
                                                 gate_span[:, 0:TCC], AF.Silu),
                                    waits=[pair_mm_g[p],
                                           hmul_done.get(p - 2, Z),
                                           sACT.ms()], tag=sACT)
                if par == 0:
                    hmul_done[p] = P.op("vector", lambda e, p=p, up_span=up_span:
                                        e.scalar_tensor_tensor(
                                            out=h_buf[:, p * TC:p * TC + TCC],
                                            in0=up_span[:, 0:TCC], scalar=0.125,
                                            in1=silu_sb[p % 2][:, 0:TCC],
                                            op0=ALU.mult, op1=ALU.mult),
                                        waits=[pair_mm_u[p], silu_done[p],
                                               sDVE.ms()], tag=sDVE)
                else:
                    hm = Z
                    for th in range(NTH):
                        hm = P.op("vector", lambda e, p=p, th=th,
                                  reg=rg_u[th % 2]: e.scalar_tensor_tensor(
                                      out=h_buf[:, p * TC + th * TH:
                                                p * TC + (th + 1) * TH],
                                      in0=reg[:, 0:TH], scalar=0.125,
                                      in1=silu_sb[p % 2][:, th * TH:(th + 1) * TH],
                                      op0=ALU.mult, op1=ALU.mult),
                                  waits=[pair_mm_u[p], silu_done[p],
                                         sDVE.ms()], tag=sDVE)
                    hmul_done[p] = hm
                pair_pe_end[Bp] = pair_mm_u[p]

        es_p4.close()
        es_xl2.close()
        es_ffn0.close()

        es_p5 = ExitStack()
        wd_sb = [es_p5.enter_context(nc.sbuf_tensor(f"wd{i}",
                                                    [128, NCH_DN * DSH], F16))
                 for i in range(2)]
        y_sb = es_p5.enter_context(nc.sbuf_tensor("y_sb", [128, NT * D], BF16))
        out_sb = [es_p5.enter_context(nc.sbuf_tensor(f"osb{i}", [128, D], F32))
                  for i in range(2)]
        x1t = [es_p5.enter_context(nc.sbuf_tensor(f"x1t{i}", [128, D], F32))
               for i in range(3)]
        ssq3 = es_p5.enter_context(nc.sbuf_tensor("ssq3", [128, 2 * NT], F32))
        rstd3 = es_p5.enter_context(nc.sbuf_tensor("rstd3", [128, NT], F32))
        sq_scr3 = es_p5.enter_context(nc.sbuf_tensor("sq_scr3", [128, D // 2],
                                                     F16))

        ph4_done = [sPE.ms(), sDVE.ms(), sACT.ms(), sGS[0].ms(), sGS[1].ms()]
        dap = cc_dng.ap().rearrange("(r c p) o -> r c p o", c=NCH_DN, p=128)
        wd_done = [Z, Z]
        wd_free = {}
        ev_list = []
        row_last = [Z] * NT
        ev_i = 0
        for rp in range(NCORES):
            w = [(cDN, 1)] + (list(ph4_done) if rp < 2 else [])
            if rp >= 2:
                w.append(wd_free[rp - 2])
            wd_done[rp % 2] = P.op("sync", lambda e, rp=rp: e.dma_start(
                out=wd_sb[rp % 2][:, :].rearrange("p (c o) -> p c o", c=NCH_DN),
                in_=dap[rp, :, :, :].transpose([1, 0, 2])),
                waits=w, tag=sWD[rp % 2], inc=16)
            last_mm = Z
            for t in range(NT):
                nds = (DSH + 511) // 512
                for dsub in range(nds):
                    N = min(512, DSH - dsub * 512)
                    reg = PS8[ev_i % 4]
                    for c in range(NCH_DN):
                        w2 = []
                        if c == 0:
                            w2 = [wd_done[rp % 2]]
                            if ev_i == 0:
                                w2 += ph4_done
                            if ev_i >= 4:
                                w2.append(ev_list[ev_i - 4])
                        last_mm = P.op("tensor", lambda e, rp=rp, t=t, c=c,
                                       reg=reg, dsub=dsub, N=N: e.matmul(
                                           reg[:, 0:N],
                                           h_buf[:, c * TC + t * 128:
                                                 c * TC + (t + 1) * 128],
                                           wd_sb[rp % 2][:, :].rearrange(
                                               "p (cc o) -> p cc o", cc=NCH_DN)
                                           [:, c, dsub * 512:dsub * 512 + N],
                                           start=(c == 0),
                                           stop=(c == NCH_DN - 1)),
                                       waits=w2, tag=sPE)
                    ev = P.op("vector", lambda e, rp=rp, t=t, reg=reg, dsub=dsub,
                              N=N: e.tensor_copy(
                                  y_sb[:, t * D + rp * DSH + dsub * 512:
                                       t * D + rp * DSH + dsub * 512 + N],
                                  reg[:, 0:N]),
                              waits=[last_mm, sDVE.ms()], tag=sDVE)
                    ev_list.append(ev)
                    ev_i += 1
                    row_last[t] = ev
            wd_free[rp] = last_mm
        out_dma = [Z] * NT
        fin_l = [Z] * NT
        for t in range(NT):
            P.op("scalar", lambda e, t=t: e.activation(
                sq_scr3[:, :], y_sb[:, t * D:t * D + D // 2], AF.Square,
                accum_out=ssq3[:, 2 * t:2 * t + 1]),
                waits=[row_last[t], sACT.ms()], tag=sACT)
            sq_b = P.op("scalar", lambda e, t=t: e.activation(
                sq_scr3[:, :], y_sb[:, t * D + D // 2:(t + 1) * D], AF.Square,
                accum_out=ssq3[:, 2 * t + 1:2 * t + 2]),
                waits=[sACT.ms()], tag=sACT)
            sq_add = P.op("vector", lambda e, t=t: e.tensor_tensor(
                ssq3[:, 2 * t:2 * t + 1], ssq3[:, 2 * t:2 * t + 1],
                ssq3[:, 2 * t + 1:2 * t + 2], ALU.add),
                waits=[sq_b, sDVE.ms()], tag=sDVE)
            P.op("scalar", lambda e, t=t: e.activation(
                rstd3[:, t:t + 1], ssq3[:, 2 * t:2 * t + 1], AF.Ln,
                bias=eps6_col[:, :], scale=1.0 / D),
                waits=[sq_add, sACT.ms()], tag=sACT)
            r3 = P.op("scalar", lambda e, t=t: e.activation(
                rstd3[:, t:t + 1], rstd3[:, t:t + 1], AF.Exp, scale=-0.5),
                waits=[sACT.ms()], tag=sACT)
            w3 = [gSP.ms()] + (list(ph4_done) if t < 3 else [])
            if t >= 3:
                w3.append(fin_l[t - 3])
            x1r = P.op("sync", lambda e, t=t: e.dma_start(
                out=x1t[t % 3][:, :], in_=x1_d.ap()[t * 128:(t + 1) * 128, :]),
                waits=w3, tag=sXR[t % 3], inc=16)
            w4 = [r3, x1r, sDVE.ms()]
            if t >= 2:
                w4.append(out_dma[t - 2])
            fin_l[t] = P.op("vector", lambda e, t=t: e.scalar_tensor_tensor(
                out=out_sb[t % 2][:, :], in0=y_sb[:, t * D:(t + 1) * D],
                scalar=rstd3[:, t:t + 1], in1=x1t[t % 3][:, :],
                op0=ALU.mult, op1=ALU.add), waits=w4, tag=sDVE)
            out_dma[t] = P.op("sync", lambda e, t=t: e.dma_start(
                out=out_d.ap()[t * 128:(t + 1) * 128, :],
                in_=out_sb[t % 2][:, :]),
                waits=[fin_l[t]], tag=sOUT[t % 2], inc=16)
        P.op("sync", None, waits=[sOUT[0].ms(), sOUT[1].ms()])

        P.emit(block)
        es_p5.close()
        es_h.close()

    es.close()
    return nc


_PROG = {}


def _get_program(ncores, tc):
    key = (ncores, tc)
    if key not in _PROG:
        _PROG[key] = build_program(ncores, tc)
    return _PROG[key]


def kernel(L0_attn_x, L0_attn_wq, L0_attn_bq, L0_attn_wk, L0_attn_bk,
           L0_attn_wv, L0_attn_bv, L0_attn_wo, L0_attn_bo,
           L0_ffn_gate_up_packed, L0_ffn_gate_up_scales,
           L0_ffn_down_packed, L0_ffn_down_scales):
    NCORES = 8
    xin = np.asarray(L0_attn_x)
    x = np.ascontiguousarray(xin.reshape(-1, D).astype(np.float32, copy=False))
    TC = x.shape[0] // NCORES
    nc = _get_program(NCORES, TC)
    gup = np.asarray(L0_ffn_gate_up_packed)
    gus = np.asarray(L0_ffn_gate_up_scales)
    dnp = np.asarray(L0_ffn_down_packed)
    dns = np.asarray(L0_ffn_down_scales)
    OSH = GO // NCORES
    DSH = D // NCORES
    wq = np.ascontiguousarray(np.asarray(L0_attn_wq))
    wk = np.ascontiguousarray(np.asarray(L0_attn_wk))
    wv = np.ascontiguousarray(np.asarray(L0_attn_wv))
    wo = np.ascontiguousarray(np.asarray(L0_attn_wo))
    bq = np.ascontiguousarray(np.asarray(L0_attn_bq))
    bk = np.ascontiguousarray(np.asarray(L0_attn_bk))
    bv = np.ascontiguousarray(np.asarray(L0_attn_bv))
    bo = np.ascontiguousarray(np.asarray(L0_attn_bo))
    in_maps = []
    for c in range(NCORES):
        in_maps.append({
            "x": np.ascontiguousarray(x[c * TC:(c + 1) * TC]),
            "wq": wq, "wk": wk, "wv": wv, "wo": wo,
            "bq": bq, "bk": bk, "bv": bv, "bo": bo,
            "gup": np.ascontiguousarray(gup[c * OSH:(c + 1) * OSH]),
            "gus": np.ascontiguousarray(gus[:, c * OSH:(c + 1) * OSH]),
            "dnp": np.ascontiguousarray(dnp[c * DSH:(c + 1) * DSH]),
            "dns": np.ascontiguousarray(dns[:, c * DSH:(c + 1) * DSH]),
        })
    res = run_bass_kernel_spmd(nc, in_maps, core_ids=list(range(NCORES)))
    out = np.concatenate([np.asarray(res.results[c]["out"])
                          for c in range(NCORES)], axis=0)
    return out.reshape(xin.shape).astype(np.float32, copy=False)


# version 15
# speedup vs baseline: 1.0648x; 1.0648x over previous
import numpy as np
import ml_dtypes
from contextlib import ExitStack

import concourse.bass as bass
import concourse.mybir as mybir
from concourse.bass_utils import run_bass_kernel_spmd

F32 = mybir.dt.float32
F16 = mybir.dt.float16
BF16 = mybir.dt.bfloat16
I32 = mybir.dt.int32
U16 = mybir.dt.uint16
AF = mybir.ActivationFunctionType
ALU = mybir.AluOpType
AX = mybir.AxisListType.X

D = 2048
NH = 16
HD = 128
IDIM = 5632
GO = 2 * IDIM
GRP = 128
SCALE_QK = float(1.0 / np.sqrt(128.0))
CSHIFT = 10.0

ENGINES = ("sync", "gpsimd", "tensor", "vector", "scalar")


class Sem:
    def __init__(self, es, nc, name, lazy=False):
        self.h = es.enter_context(nc.semaphore(name))
        self.n = 0
        self.lazy = lazy
        self.refs = set()
        self.rank = None

    def ms(self):
        return (self, self.n)


class Plan:

    def __init__(self):
        self.ops = {e: [] for e in ENGINES}

    def op(self, eng, fn, waits=(), tag=None, inc=1):
        ws = [(s, v) for (s, v) in waits if s is not None and v > 0]
        ln = 0
        if tag is not None:
            tag.n += inc
            ln = tag.n
        self.ops[eng].append((ws, fn, tag, inc, ln))
        if tag is not None:
            return (tag, tag.n)
        return None

    def emit(self, block):
        lazy = set()
        for eng in ENGINES:
            for ws, fn, tag, inc, ln in self.ops[eng]:
                for s, v in ws:
                    if s.lazy:
                        s.refs.add(v)
                        lazy.add(s)
                if tag is not None and tag.lazy:
                    lazy.add(tag)
        for s in lazy:
            s.rank = {v: i + 1 for i, v in enumerate(sorted(s.refs))}
        for eng in ENGINES:
            lst = self.ops[eng]
            if not lst:
                continue

            def make(lst):
                def run(e):
                    for ws, fn, tag, inc, ln in lst:
                        for s, v in ws:
                            e.wait_ge(s.h, s.rank[v] if s.lazy else v)
                        if fn is None:
                            continue
                        ins = fn(e)
                        if tag is not None:
                            if tag.lazy:
                                if ln in tag.refs:
                                    ins.then_inc(tag.h, 1)
                            else:
                                ins.then_inc(tag.h, inc)
                return run
            getattr(block, eng)(make(lst))
        self.ops = {e: [] for e in ENGINES}


Z = (None, 0)


def build_program(NCORES, TC):
    S = NCORES * TC
    NT = TC // 128
    KT = S // 128
    QTW = min(512, TC)
    QT = TC // QTW
    NQS = QTW // 128
    NTH = max(1, TC // 512)
    TH = TC // NTH
    OSH = GO // NCORES
    DSH = D // NCORES
    NB_D = D // 512
    NB_I = IDIM // 512
    NCH_DN = 4 * NB_I
    DNW_U = IDIM // 4 // 128
    NSG = D // GRP
    NSD = IDIM // GRP
    groups = []
    k0 = 0
    while k0 < KT:
        ln = min(3, KT - k0)
        groups.append((k0, ln))
        k0 += ln
    NG = len(groups)
    NITS = NH * QT
    VSPL = max(1, KT // 16)
    KT_PER_V = (KT + VSPL - 1) // VSPL

    nc = bass.Bass("TRN2", target_bir_lowering=False)

    x_d = nc.dram_tensor("x", [TC, D], F32, kind="ExternalInput")
    wq_d = nc.dram_tensor("wq", [D, D], F32, kind="ExternalInput")
    wk_d = nc.dram_tensor("wk", [D, D], F32, kind="ExternalInput")
    wv_d = nc.dram_tensor("wv", [D, D], F32, kind="ExternalInput")
    wo_d = nc.dram_tensor("wo", [D, D], F32, kind="ExternalInput")
    b_d = [nc.dram_tensor(n, [D], F32, kind="ExternalInput")
           for n in ("bq", "bk", "bv", "bo")]
    gup_d = nc.dram_tensor("gup", [OSH, D // 8], I32, kind="ExternalInput")
    gus_d = nc.dram_tensor("gus", [NSG, OSH], F16, kind="ExternalInput")
    dnp_d = nc.dram_tensor("dnp", [DSH, IDIM // 8], I32, kind="ExternalInput")
    dns_d = nc.dram_tensor("dns", [NSD, DSH], F16, kind="ExternalInput")
    out_d = nc.dram_tensor("out", [TC, D], F32, kind="ExternalOutput")

    cc_k_in = nc.dram_tensor("cc_k_in", [D, TC], BF16)
    cc_kg = [nc.dram_tensor(f"cc_kg{i}", [NCORES * (D // 2), TC], BF16)
             for i in range(2)]
    cc_v_in = nc.dram_tensor("cc_v_in", [2 * TC, D // 2], BF16)
    cc_vg = [nc.dram_tensor(f"cc_vg{i}", [S, D // 2], BF16) for i in range(2)]
    cc_gu_in = nc.dram_tensor("cc_gu_in", [D, OSH], F16)
    cc_gug = nc.dram_tensor("cc_gug", [NCORES * D, OSH], F16)
    cc_dn_in = nc.dram_tensor("cc_dn_in", [IDIM, DSH], F16)
    cc_dng = nc.dram_tensor("cc_dng", [NCORES * IDIM, DSH], F16)
    x1_d = nc.dram_tensor("x1spill", [TC, D], F32)
    rg = [list(range(NCORES))]

    es = ExitStack()
    sPE = Sem(es, nc, "sPE", lazy=True)
    sACT = Sem(es, nc, "sACT", lazy=True)
    sDVE = Sem(es, nc, "sDVE", lazy=True)
    sPOOL = Sem(es, nc, "sPOOL", lazy=True)
    cK = [Sem(es, nc, f"cK{i}") for i in range(2)]
    cV = [Sem(es, nc, f"cV{i}") for i in range(2)]
    cGU = Sem(es, nc, "cGU")
    cDN = Sem(es, nc, "cDN")
    sPK = Sem(es, nc, "sPK")
    sSC = Sem(es, nc, "sSC")
    sX = [Sem(es, nc, f"sX{i}") for i in range(2)]
    sK = [Sem(es, nc, f"sK{i}") for i in range(2)]
    sV = [Sem(es, nc, f"sV{i}") for i in range(2)]
    sCT = [Sem(es, nc, f"sCT{i}") for i in range(2)]
    sGS = [Sem(es, nc, f"sGS{i}") for i in range(2)]
    sWD = [Sem(es, nc, f"sWD{i}") for i in range(2)]
    sXR = [Sem(es, nc, f"sXR{i}") for i in range(3)]
    sOUT = [Sem(es, nc, f"sOUT{i}") for i in range(4)]
    gW = [Sem(es, nc, f"gW{i}") for i in range(3)]
    gB = Sem(es, nc, "gB")
    gBV = Sem(es, nc, "gBV")
    gK = [Sem(es, nc, f"gK{i}") for i in range(6)]
    gV = [Sem(es, nc, f"gV{i}") for i in range(6)]
    gDQ = [Sem(es, nc, f"gDQ{i}") for i in range(2)]
    gWO = [Sem(es, nc, f"gWO{i}") for i in range(4)]
    gSP = Sem(es, nc, "gSP")

    stA = es.enter_context(nc.psum_tensor("stA", [128, 1536], F32))
    stB = es.enter_context(nc.psum_tensor("stB", [128, 1536], F32))
    ctxAB = es.enter_context(nc.psum_tensor("ctxAB", [128, 1024], F32))
    stA_bf = stA.bitcast(BF16)
    stB_bf = stB.bitcast(BF16)
    ctx_bf = ctxAB.bitcast(BF16)
    PS8 = [stA[:, 0:512], stA[:, 512:1024], stA[:, 1024:1536],
           stB[:, 0:512], stB[:, 512:1024], stB[:, 1024:1536],
           ctxAB[:, 0:512], ctxAB[:, 512:1024]]
    TRREG = [stA_bf[:, 0:128], stA_bf[:, 1024:1152], stA_bf[:, 2048:2176],
             stB_bf[:, 0:128], stB_bf[:, 1024:1152], stB_bf[:, 2048:2176]]

    id_sb = es.enter_context(nc.sbuf_tensor("id_sb", [128, 128], BF16))
    bias_cols = es.enter_context(nc.sbuf_tensor("bias_cols", [128, 64], F32))
    c10_col = es.enter_context(nc.sbuf_tensor("c10_col", [128, 1], F32))
    eps5_col = es.enter_context(nc.sbuf_tensor("eps5_col", [128, 1], F32))
    eps6_col = es.enter_context(nc.sbuf_tensor("eps6_col", [128, 1], F32))
    ones1 = es.enter_context(nc.sbuf_tensor("ones1", [1, 1], F32))
    onesr = es.enter_context(nc.sbuf_tensor("onesr", [1, 128], F32))

    P = Plan()

    with nc.Block() as block:
        es_deq = ExitStack()
        gupT = [es_deq.enter_context(nc.sbuf_tensor(f"gupT{b}", [128, OSH], U16,
                                                    is_transpose_dest=True))
                for b in range(4)]
        dnpT = [es_deq.enter_context(nc.sbuf_tensor(f"dnpT{b}", [128, DSH], U16,
                                                    is_transpose_dest=True))
                for b in range(DNW_U)]
        scT_gu = es_deq.enter_context(nc.sbuf_tensor("scT_gu", [128, NB_D * OSH],
                                                     F16))
        scT_dn = es_deq.enter_context(nc.sbuf_tensor("scT_dn", [128, NB_I * DSH],
                                                     F16))
        DQW = max(OSH // 2, DSH)
        dq_scr = [es_deq.enter_context(nc.sbuf_tensor(f"dqs{i}", [128, DQW], U16))
                  for i in range(4)]
        dq_out = [es_deq.enter_context(nc.sbuf_tensor(f"dqo{i}", [128, DQW], F16))
                  for i in range(2)]

        es_q = ExitStack()
        q_fm = es_q.enter_context(nc.sbuf_tensor("q_fm", [128, NH * TC], BF16))

        es_sc = ExitStack()
        gus_sb = es_sc.enter_context(nc.sbuf_tensor("gus_sb", [NSG, OSH], F16))
        dns_sb = es_sc.enter_context(nc.sbuf_tensor("dns_sb", [NSD, DSH], F16))
        ind_gu = es_sc.enter_context(nc.sbuf_tensor("ind_gu", [NSG, NB_D * 128],
                                                    F16))
        ind_dn = es_sc.enter_context(nc.sbuf_tensor("ind_dn", [NSD, NB_I * 128],
                                                    F16))

        es_p1 = ExitStack()
        xbuf = [es_p1.enter_context(nc.sbuf_tensor(f"xbuf{i}", [128, D], F32))
                for i in range(2)]
        xl_tok = [es_p1.enter_context(nc.sbuf_tensor(f"xlt{i}", [128, D], BF16))
                  for i in range(2)]
        xl_fm = es_p1.enter_context(nc.sbuf_tensor("xl_fm", [128, 16 * TC], BF16))
        wsl = [es_p1.enter_context(nc.sbuf_tensor(f"wsl{i}", [128, 16 * 512],
                                                  BF16)) for i in range(3)]
        kev = [es_p1.enter_context(nc.sbuf_tensor(f"kev{i}", [128, 512], BF16))
               for i in range(6)]
        sums = es_p1.enter_context(nc.sbuf_tensor("sums", [128, NT], F32))
        ssq = es_p1.enter_context(nc.sbuf_tensor("ssq", [128, 2 * NT], F32))
        mcol = es_p1.enter_context(nc.sbuf_tensor("mcol", [128, NT], F32))
        rstd = es_p1.enter_context(nc.sbuf_tensor("rstd", [128, NT], F32))
        sq_scr = es_p1.enter_context(nc.sbuf_tensor("sq_scr", [128, D // 2],
                                                    F16))
        brow = es_p1.enter_context(nc.sbuf_tensor("brow", [1, 512], F32))
        onesr_bf = es_p1.enter_context(nc.sbuf_tensor("onesr_bf", [1, 128],
                                                      BF16))
        bvr = [es_p1.enter_context(nc.sbuf_tensor(f"bvr{i}", [1, 512], BF16))
               for i in range(2)]

        idz = P.op("gpsimd", lambda e: e.memset(id_sb[:, :], 0.0), tag=sPOOL)
        P.op("gpsimd", lambda e: e.memset(c10_col[:, :], -CSHIFT), tag=sPOOL)
        P.op("gpsimd", lambda e: e.memset(eps5_col[:, :], 1e-5), tag=sPOOL)
        P.op("gpsimd", lambda e: e.memset(eps6_col[:, :], 1e-6 / 64.0), tag=sPOOL)
        P.op("gpsimd", lambda e: e.memset(ones1[:, :], 1.0), tag=sPOOL)
        P.op("gpsimd", lambda e: e.memset(onesr[:, :], 1.0), tag=sPOOL)
        ones_bf_done = P.op("gpsimd", lambda e: e.memset(onesr_bf[:, :], 1.0),
                            tag=sPOOL)
        id_done = P.op("gpsimd", lambda e: e.affine_select(
            out=id_sb[:, :], in_=id_sb[:, :], compare_op=ALU.not_equal, fill=1.0,
            base=0, pattern=[[-1, 128]], channel_multiplier=1),
            waits=[idz], tag=sPOOL)
        P.op("gpsimd", lambda e: e.memset(ind_gu[:, :], 0.0), tag=sPOOL)
        indz = P.op("gpsimd", lambda e: e.memset(ind_dn[:, :], 0.0), tag=sPOOL)
        ind_done = None
        for nb, ind in ((NB_D, ind_gu), (NB_I, ind_dn)):
            v3 = ind[:, :].rearrange("s (g w) -> s g w", w=32)
            ind_done = P.op("gpsimd", lambda e, v3=v3, nb=nb: e.affine_select(
                out=v3, in_=v3, compare_op=ALU.not_equal, fill=1.0, base=0,
                pattern=[[-1, nb * 4], [0, 32]], channel_multiplier=1),
                waits=[indz], tag=sPOOL)

        P.op("sync", lambda e: e.dma_start(out=gus_sb[:, :], in_=gus_d.ap()),
             tag=sSC, inc=16)
        scales_done = P.op("sync", lambda e: e.dma_start(out=dns_sb[:, :],
                                                         in_=dns_d.ap()),
                           tag=sSC, inc=16)
        gup_u = gup_d.bitcast(U16)
        packed_done = None
        for b in range(4):
            packed_done = P.op("sync", lambda e, b=b: e.dma_start_transpose(
                gupT[b][:, :], gup_u.ap()[:, b * 128:(b + 1) * 128]),
                tag=sPK, inc=16)
        dnp_u = dnp_d.bitcast(U16)
        for b in range(DNW_U):
            packed_done = P.op("sync", lambda e, b=b: e.dma_start_transpose(
                dnpT[b][:, :], dnp_u.ap()[:, b * 128:(b + 1) * 128]),
                tag=sPK, inc=16)

        x_done = [Z] * NT
        ln_apply = [Z] * NT
        ln_sq = [Z] * NT
        xl_tr_last = [Z] * NT
        xl_evac = [Z] * (NT * 16)

        def plan_xload(t):
            w = []
            if t >= 2:
                w = [ln_apply[t - 2], ln_sq[t - 2]]
            x_done[t] = P.op("sync", lambda e, t=t: e.dma_start(
                out=xbuf[t % 2][:, :], in_=x_d.ap()[t * 128:(t + 1) * 128, :]),
                waits=w, tag=sX[t % 2], inc=16)

        plan_xload(0)
        if NT > 1:
            plan_xload(1)
        for t in range(NT):
            xb = xbuf[t % 2]
            P.op("vector", lambda e, xb=xb, t=t: e.reduce_sum(
                sums[:, t:t + 1], xb[:, :], axis=AX), waits=[x_done[t]], tag=sDVE)
            P.op("vector", lambda e, t=t: e.tensor_scalar(
                mcol[:, t:t + 1], sums[:, t:t + 1], 1.0 / D, None, ALU.mult),
                waits=[sDVE.ms()], tag=sDVE)
            P.op("scalar", lambda e, xb=xb, t=t: e.activation(
                sq_scr[:, :], xb[:, 0:D // 2], AF.Square,
                accum_out=ssq[:, 2 * t:2 * t + 1]),
                waits=[x_done[t], sACT.ms()], tag=sACT)
            sq_b1 = P.op("scalar", lambda e, xb=xb, t=t: e.activation(
                sq_scr[:, :], xb[:, D // 2:D], AF.Square,
                accum_out=ssq[:, 2 * t + 1:2 * t + 2]),
                waits=[sACT.ms()], tag=sACT)
            ln_sq[t] = P.op("vector", lambda e, t=t: e.tensor_tensor(
                ssq[:, 2 * t:2 * t + 1], ssq[:, 2 * t:2 * t + 1],
                ssq[:, 2 * t + 1:2 * t + 2], ALU.add),
                waits=[sq_b1, sDVE.ms()], tag=sDVE)
            m2d = P.op("vector", lambda e, t=t: e.tensor_scalar(
                sums[:, t:t + 1], mcol[:, t:t + 1], mcol[:, t:t + 1], None,
                ALU.mult), waits=[sDVE.ms()], tag=sDVE)
            vard = P.op("vector", lambda e, t=t: e.scalar_tensor_tensor(
                out=rstd[:, t:t + 1], in0=ssq[:, 2 * t:2 * t + 1],
                scalar=1.0 / D, in1=sums[:, t:t + 1], op0=ALU.mult,
                op1=ALU.subtract),
                waits=[m2d, ln_sq[t], sDVE.ms()], tag=sDVE)
            P.op("scalar", lambda e, t=t: e.activation(
                rstd[:, t:t + 1], rstd[:, t:t + 1], AF.Ln, bias=eps5_col[:, :]),
                waits=[vard, sACT.ms(), id_done], tag=sACT)
            rstd_done = P.op("scalar", lambda e, t=t: e.activation(
                rstd[:, t:t + 1], rstd[:, t:t + 1], AF.Exp, scale=-0.5),
                waits=[sACT.ms()], tag=sACT)
            w = [rstd_done, sDVE.ms()]
            if t >= 2:
                w.append(xl_tr_last[t - 2])
            ln_apply[t] = P.op("vector", lambda e, xb=xb, t=t: e.tensor_scalar(
                xl_tok[t % 2][:, :], xb[:, :], mcol[:, t:t + 1], rstd[:, t:t + 1],
                ALU.subtract, ALU.mult), waits=w, tag=sDVE)
            for dc in range(16):
                i = t * 16 + dc
                r = i % 6
                w = [ln_apply[t]]
                if i >= 6:
                    w.append(xl_evac[i - 6])
                xl_tr_last[t] = P.op("tensor", lambda e, t=t, dc=dc, r=r:
                                     e.transpose(
                                         TRREG[r][:, :],
                                         xl_tok[t % 2][:, dc * 128:(dc + 1) * 128],
                                         id_sb[:, :]), waits=w, tag=sPE)
                xl_evac[i] = P.op("vector", lambda e, t=t, dc=dc, r=r:
                                  e.tensor_copy(
                                      xl_fm[:, dc * TC + t * 128:
                                            dc * TC + (t + 1) * 128],
                                      TRREG[r][:, :]), waits=[sPE.ms()], tag=sDVE)
            if t + 2 < NT:
                plan_xload(t + 2)
        xl_all = xl_evac[-1]

        def w_cast(e, dst, src, q4):
            return e.dma_start(
                out=dst[:, :].rearrange("p (c o) -> p c o", c=16),
                in_=src.ap()[:, q4 * 512:(q4 + 1) * 512]
                .rearrange("(c p) o -> p c o", p=128))

        k_dma = [Z] * 32
        v_evac = [Z] * 32
        v_dma_u = [Z] * 32
        grp_evac = {}
        grp_mm = {}
        slab_ready = {}

        def plan_slab(Q):
            pi, q4 = Q // 4, Q % 4
            wslab = wsl[Q % 3]
            wsrc = [wk_d, wv_d, wq_d][pi]
            w = []
            if Q >= 3:
                gprev = (Q - 3) * 2 + 1
                w.append(grp_mm[(gprev // 8, gprev % 8)])
            slab_ready[Q] = P.op("gpsimd", lambda e, wslab=wslab, wsrc=wsrc,
                                 q4=q4: w_cast(e, wslab, wsrc, q4),
                                 waits=w, tag=gW[Q % 3], inc=16)

        plan_slab(0)
        plan_slab(1)
        plan_slab(2)
        bias_last = Z
        for i in range(4):
            for qq in range(4):
                w = [bias_last] if (i or qq) else []
                brow_done = P.op("sync", lambda e, i=i, qq=qq: e.dma_start(
                    out=brow[:, :],
                    in_=b_d[i].ap()[qq * 512:(qq + 1) * 512][None, :]),
                    waits=w, tag=gB, inc=16)
                for ol in range(4):
                    oc = qq * 4 + ol
                    bias_last = P.op("tensor", lambda e, i=i, oc=oc, ol=ol:
                                     e.matmul(
                                         ctxAB[:, i * 16 + oc:i * 16 + oc + 1],
                                         brow[:, ol * 128:(ol + 1) * 128],
                                         ones1[:, :], start=True, stop=True),
                                     waits=([brow_done, id_done] if ol == 0
                                            else []), tag=sPE)
        bias_evac = P.op("vector", lambda e: e.tensor_copy(
            bias_cols[:, :], ctxAB[:, 0:64]), waits=[bias_last], tag=sDVE)

        PROJ_ORDER = [1, 2, 0]
        cc_issued = [False, False]
        def issue_half(kind, half):
            if kind == "K":
                ins_ = cc_k_in.ap()[half * (D // 2):(half + 1) * (D // 2), :]
                outs_ = cc_kg[half].ap()
                sem, dep = cK[half], gK
            else:
                ins_ = cc_v_in.ap()[half * TC:(half + 1) * TC, :]
                outs_ = cc_vg[half].ap()
                sem, dep = cV[half], gV
            P.op("gpsimd", lambda e, ins_=ins_, outs_=outs_: e.collective_compute(
                "AllGather", ALU.bypass, replica_groups=rg,
                ins=[ins_.opt()], outs=[outs_.opt()]),
                waits=[s.ms() for s in dep], tag=sem)

        def plan_v_flip():
            bv_done_q = [Z] * 4
            for q4 in range(4):
                Q = 4 + q4
                wslab = wsl[Q % 3]
                wb = [grp_mm[(1, 2 * (q4 - 2) + 1)]] if q4 >= 2 else []
                bv_done_q[q4] = P.op("gpsimd", lambda e, q4=q4: e.dma_start(
                    out=bvr[q4 % 2][:, :],
                    in_=b_d[2].ap()[q4 * 512:(q4 + 1) * 512][None, :]),
                    waits=wb, tag=gBV, inc=16)
                for t in range(NT):
                    vu = q4 * NT + t
                    reg = PS8[vu % 8]
                    w = []
                    if t == 0:
                        w.append(slab_ready[Q])
                    if vu == 0:
                        w += [ones_bf_done, bias_evac]
                    if vu < 8:
                        w.append(grp_evac[(0, 6 if vu < 4 else 7)])
                    else:
                        w.append(v_evac[vu - 8])
                    for dc in range(16):
                        P.op("tensor", lambda e, wslab=wslab, dc=dc, t=t,
                             reg=reg: e.matmul(
                                 reg[:, 0:512],
                                 xl_fm[:, dc * TC + t * 128:
                                       dc * TC + (t + 1) * 128],
                                 wslab[:, dc * 512:(dc + 1) * 512],
                                 start=(dc == 0), stop=False),
                             waits=(w if dc == 0 else []), tag=sPE)
                    last = P.op("tensor", lambda e, q4=q4, reg=reg: e.matmul(
                        reg[:, 0:512], onesr_bf[:, :], bvr[q4 % 2][:, :],
                        start=False, stop=True),
                        waits=([bv_done_q[q4]] if t == 0 else []), tag=sPE)
                    grp_mm[(1, 2 * q4 + 1)] = last
                    vv = kev[(32 + vu) % 6]
                    we = [last, sDVE.ms()]
                    prevq = 32 + vu - 6
                    if prevq >= 32:
                        we.append(v_dma_u[prevq - 32])
                    else:
                        we.append(k_dma[prevq])
                    v_evac[vu] = P.op("vector", lambda e, vv=vv, reg=reg:
                                      e.tensor_copy(vv[:, 0:512],
                                                    reg[:, 0:512]),
                                      waits=we, tag=sDVE)
                    hf, col = q4 // 2, (q4 % 2) * 512
                    v_dma_u[vu] = P.op("gpsimd", lambda e, vv=vv, hf=hf, t=t,
                                       col=col: e.dma_start(
                                           out=cc_v_in.ap()[
                                               hf * TC + t * 128:
                                               hf * TC + (t + 1) * 128,
                                               col:col + 512],
                                           in_=vv[:, 0:512]),
                                       waits=[v_evac[vu]], tag=gV[vu % 6],
                                       inc=16)
                plan_slab(Q + 3)
                if q4 == 1:
                    issue_half("K", 1)
            grp_evac[(1, 6)] = v_evac[3 * NT + 3]
            grp_evac[(1, 7)] = v_evac[4 * NT - 1]
            issue_half("V", 0)

        for pi in range(3):
            proj = PROJ_ORDER[pi]
            if pi == 1:
                issue_half("K", 0)
                plan_v_flip()
                continue
            for g in range(8):
                if g == 4 and pi == 2:
                    issue_half("V", 1)
                Q = pi * 4 + g // 2
                wslab = wsl[Q % 3]
                par = g % 2
                regs = PS8[par * 4:par * 4 + 4]
                for dc in range(16):
                    for oc2 in range(2):
                        oc = g * 2 + oc2
                        ol = oc % 4
                        for th in range(NTH):
                            w = []
                            if dc == 0 and oc2 == 0 and th == 0:
                                w = [xl_all, slab_ready[Q], bias_evac]
                                prev = (pi * 8 + g) - 2
                                if prev >= 0:
                                    w.append(grp_evac[(prev // 8, prev % 8)])
                            grp_mm[(pi, g)] = P.op(
                                "tensor",
                                lambda e, wslab=wslab, dc=dc, ol=ol,
                                reg=regs[oc2 * 2 + (th % 2)], th=th: e.matmul(
                                    reg[:, 0:TH],
                                    wslab[:, dc * 512 + ol * 128:
                                          dc * 512 + (ol + 1) * 128],
                                    xl_fm[:, dc * TC + th * TH:
                                          dc * TC + (th + 1) * TH],
                                    start=(dc == 0), stop=(dc == 15)),
                                waits=w, tag=sPE)
                for oc2 in range(2):
                    oc = g * 2 + oc2
                    for th in range(NTH):
                        reg = regs[oc2 * 2 + (th % 2)]
                        bcol = bias_cols[:, proj * 16 + oc:proj * 16 + oc + 1]
                        u = (g * 2 + oc2) * NTH + th
                        if proj == 0:
                            dst = q_fm[:, oc * TC + th * TH:oc * TC + (th + 1) * TH]
                            grp_evac[(pi, g)] = P.op(
                                "vector", lambda e, dst=dst, reg=reg, bcol=bcol:
                                e.tensor_scalar(dst, reg[:, 0:TH], bcol, None,
                                                ALU.add),
                                waits=[grp_mm[(pi, g)], sDVE.ms()], tag=sDVE)
                        elif proj == 1:
                            kv = kev[u % 6]
                            w = [grp_mm[(pi, g)], sDVE.ms()]
                            if u >= 6:
                                w.append(k_dma[u - 6])
                            grp_evac[(pi, g)] = P.op(
                                "vector", lambda e, kv=kv, reg=reg, bcol=bcol:
                                e.tensor_scalar(kv[:, 0:TH], reg[:, 0:TH], bcol,
                                                None, ALU.add), waits=w, tag=sDVE)
                            k_dma[u] = P.op("gpsimd", lambda e, kv=kv, oc=oc,
                                            th=th: e.dma_start(
                                                out=cc_k_in.ap()[
                                                    oc * 128:(oc + 1) * 128,
                                                    th * TH:(th + 1) * TH],
                                                in_=kv[:, 0:TH]),
                                            waits=[grp_evac[(pi, g)]],
                                            tag=gK[u % 6], inc=16)
                if g % 2 == 1 and Q + 3 < 12:
                    plan_slab(Q + 3)
        qkv_evac_all = grp_evac[(2, 7)]

        reg_i = 0
        sc_evac = []
        for kind in range(2):
            nb = NB_D if kind == 0 else NB_I
            wdt_tot = OSH if kind == 0 else DSH
            ind = ind_gu if kind == 0 else ind_dn
            scf = gus_sb if kind == 0 else dns_sb
            sct = scT_gu if kind == 0 else scT_dn
            for b in range(nb):
                for o0 in range(0, wdt_tot, 512):
                    wdt = min(512, wdt_tot - o0)
                    reg = PS8[reg_i % 4]
                    w = [scales_done, ind_done, qkv_evac_all]
                    if reg_i >= 4:
                        w.append(sc_evac[reg_i - 4])
                    mm = P.op("tensor", lambda e, b=b, o0=o0, wdt=wdt, reg=reg,
                              ind=ind, scf=scf: e.matmul(
                                  reg[:, 0:wdt], ind[:, b * 128:(b + 1) * 128],
                                  scf[:, o0:o0 + wdt], start=True, stop=True),
                              waits=w, tag=sPE)
                    sc_evac.append(P.op("vector", lambda e, b=b, o0=o0, wdt=wdt,
                                        reg=reg, sct=sct, wt=wdt_tot:
                                        e.tensor_copy(
                                            sct[:, b * wt + o0:
                                                b * wt + o0 + wdt],
                                            reg[:, 0:wdt]),
                                        waits=[mm, sDVE.ms()], tag=sDVE))
                    reg_i += 1
        scT_done = sc_evac[-1]

        es_bo = ExitStack()
        bo_bcast = es_bo.enter_context(nc.sbuf_tensor("bo_bcast", [128, D], F32,
                                                      side="right"))
        bo_evac = Z
        bo_mm = bias_last
        for ds_ in range(4):
            reg = PS8[4 + ds_ % 2]
            bo_row_done = P.op("sync", lambda e, ds_=ds_: e.dma_start(
                out=brow[:, :],
                in_=b_d[3].ap()[ds_ * 512:(ds_ + 1) * 512][None, :]),
                waits=[bo_mm], tag=gB, inc=16)
            w = [bo_row_done, qkv_evac_all]
            if ds_ >= 2:
                w.append(bo_evac)
            bo_mm = P.op("tensor", lambda e, reg=reg: e.matmul(
                reg[:, 0:512], onesr[:, :], brow[:, 0:512],
                start=True, stop=True), waits=w, tag=sPE)
            bo_evac = P.op("vector", lambda e, ds_=ds_, reg=reg: e.tensor_copy(
                bo_bcast[:, ds_ * 512:(ds_ + 1) * 512], reg[:, 0:512]),
                waits=[bo_mm, sDVE.ms()], tag=sDVE)

        es_p1.close()
        es_sc.close()

        es_ctx = ExitStack()
        ctx_fm = es_ctx.enter_context(nc.sbuf_tensor("ctx_fm", [128, NH * TC],
                                                     BF16, is_transpose_dest=True,
                                                     side="right"))
        es_wo0 = ExitStack()
        wo_cb0 = es_wo0.enter_context(nc.sbuf_tensor("wocb0", [128, 16 * 512],
                                                     BF16, side="right"))
        es_att = ExitStack()
        K_buf = [es_att.enter_context(nc.sbuf_tensor(f"Kb{i}", [128, S], BF16))
                 for i in range(2)]
        V_buf = [es_att.enter_context(nc.sbuf_tensor(f"Vb{i}", [128, KT * 129],
                                                     BF16)) for i in range(2)]
        Pslab = [es_att.enter_context(nc.sbuf_tensor(f"Ps{i}", [128, 3 * QTW],
                                                     BF16)) for i in range(2)]
        ctx_n = [es_att.enter_context(nc.sbuf_tensor(f"cn{i}", [128, 128], BF16))
                 for i in range(2 * NQS)]
        rcp = es_att.enter_context(nc.sbuf_tensor("rcp", [128, 2 * NQS], F32))

        ph2_pre = ([sPE.ms(), sDVE.ms(), sACT.ms(), gB.ms(), gBV.ms()]
                   + [s.ms() for s in gK] + [s.ms() for s in gV]
                   + [s.ms() for s in gW])
        vone = [Z, Z]
        for i in range(2):
            vone[i] = P.op("gpsimd", lambda e, i=i: e.memset(
                V_buf[i][:, :].rearrange("p (k e) -> p k e", e=129)[:, :, 128:129],
                1.0), waits=ph2_pre, tag=sPOOL)
        wo0_ready = P.op("gpsimd", lambda e: e.dma_start(
            out=wo_cb0[:, :].rearrange("p (c o) -> p c o", c=16),
            in_=wo_d.ap()[:, 0:512].rearrange("(c p) o -> p c o", p=128)),
            waits=ph2_pre, tag=gWO[0], inc=16)

        kap = [cc_kg[i].ap().rearrange("(r hh p) t -> r hh p t", hh=NH // 2,
                                       p=128) for i in range(2)]
        vap = [cc_vg[i].ap().rearrange("(kt p) (hh dd) -> kt p hh dd", p=128,
                                       dd=128) for i in range(2)]
        kv_load = [None] * NH

        def plan_kv_load(h, after_pe):
            buf = h % 2
            hf, hl = h // (NH // 2), h % (NH // 2)
            w = [(cK[hf], 1)]
            if after_pe != Z:
                w.append(after_pe)
            else:
                w += ph2_pre
            kc = P.op("sync", lambda e, hf=hf, hl=hl, buf=buf: e.dma_start(
                out=K_buf[buf][:, :].rearrange("p (r t) -> p r t", r=NCORES),
                in_=kap[hf][:, hl, :, :].transpose([1, 0, 2])),
                waits=w, tag=sK[buf], inc=16)
            vc = Z
            for sp in range(VSPL):
                kt0 = sp * KT_PER_V
                kt1 = min(KT, kt0 + KT_PER_V)
                w2 = []
                if sp == 0:
                    w2 = [(cV[hf], 1), vone[buf]]
                    if after_pe != Z:
                        w2.append(after_pe)
                    else:
                        w2 += ph2_pre
                vc = P.op("sync", lambda e, hf=hf, hl=hl, buf=buf, kt0=kt0,
                          kt1=kt1: e.dma_start(
                              out=V_buf[buf][:, :].rearrange(
                                  "p (k e) -> p k e", e=129)[:, kt0:kt1, 0:128],
                              in_=vap[hf][:, :, hl, :].transpose([1, 0, 2])
                              [:, kt0:kt1, :]),
                          waits=w2, tag=sV[buf], inc=16)
            kv_load[h] = (kc, vc)

        plan_kv_load(0, Z)
        if NH > 1:
            plan_kv_load(1, Z)

        dq_chunks = ([("gu", b, g, hf) for b in range(4) for g in range(4)
                      for hf in range(2)] +
                     [("dn", b, g, 0) for b in range(NB_I) for g in range(4)])
        n_dq = len(dq_chunks)
        dq_dma = [Z] * n_dq

        def plan_dq_chunk(ci):
            kind, b, g, hf = dq_chunks[ci]
            if kind == "gu":
                wdt = OSH // 2
                src = gupT[b][:, hf * wdt:(hf + 1) * wdt]
                sct = scT_gu[:, b * OSH + hf * wdt:b * OSH + (hf + 1) * wdt]
                cdst, row, col0 = cc_gu_in, (4 * b + g) * 128, hf * wdt
            else:
                wdt = DSH
                src = dnpT[b][:, 0:wdt]
                sct = scT_dn[:, b * DSH:(b + 1) * DSH]
                cdst, row, col0 = cc_dn_in, (4 * b + g) * 128, 0
            em2, s2, t1, t2 = [t[:, 0:wdt] for t in dq_scr]
            raw = em2
            out = dq_out[ci % 2][:, 0:wdt]
            m_em, m_s = 7 << (4 * g), 8 << (4 * g)
            sh, shs = 9 - 4 * g, 12 - 4 * g
            w0 = [packed_done, scT_done, sDVE.ms()]
            if ci >= 2:
                w0.append(dq_dma[ci - 2])

            def step(fn, w=None):
                return P.op("vector", fn, waits=(w or [sDVE.ms()]), tag=sDVE)
            if sh >= 0:
                step(lambda e: e.tensor_scalar(em2, src, m_em, sh,
                                               ALU.bitwise_and,
                                               ALU.logical_shift_left), w0)
            else:
                step(lambda e: e.tensor_scalar(em2, src, m_em, -sh,
                                               ALU.bitwise_and,
                                               ALU.logical_shift_right), w0)
            step(lambda e: e.tensor_scalar(em2, em2, 0x3800, None, ALU.add))
            if shs > 0:
                step(lambda e: e.tensor_scalar(s2, src, m_s, shs,
                                               ALU.bitwise_and,
                                               ALU.logical_shift_left))
            else:
                step(lambda e: e.tensor_scalar(s2, src, m_s, None,
                                               ALU.bitwise_and))
            rawf = raw.bitcast(F16)
            t1f, t2f = t1.bitcast(F16), t2.bitcast(F16)
            step(lambda e: e.tensor_scalar(t1f, rawf, 2.0, -1.0, ALU.mult,
                                           ALU.add))
            step(lambda e: e.tensor_scalar(t2f, rawf, -0.5, None, ALU.add))
            step(lambda e: e.tensor_tensor(t1f, t1f, t2f, ALU.max))
            step(lambda e: e.tensor_tensor(t1f, rawf, t1f, ALU.min))
            if kind == "gu":
                K4 = wdt // 4
                last = Z
                for g4 in range(4):
                    step(lambda e, g4=g4: e.tensor_tensor(
                        out.bitcast(U16)[:, g4 * K4:(g4 + 1) * K4],
                        t1[:, g4::4], s2[:, g4::4], ALU.bitwise_or))
                    last = step(lambda e, g4=g4: e.tensor_tensor(
                        out[:, g4 * K4:(g4 + 1) * K4],
                        out[:, g4 * K4:(g4 + 1) * K4],
                        sct[:, g4::4], ALU.mult))
            else:
                step(lambda e: e.tensor_tensor(out.bitcast(U16), t1, s2,
                                               ALU.bitwise_or))
                last = step(lambda e: e.tensor_tensor(out, out, sct, ALU.mult))
            if kind == "gu":
                K4 = wdt // 4
                dst = cc_gu_in.ap()[row:row + 128, :].rearrange(
                    "r (g k) -> r g k", g=4)[:, :, (col0 // 4):(col0 // 4) + K4]
                dq_dma[ci] = P.op("gpsimd", lambda e, out=out, dst=dst, K4=K4:
                                  e.dma_start(
                                      out=dst,
                                      in_=out.rearrange("p (g k) -> p g k", g=4)),
                                  waits=[last], tag=gDQ[ci % 2], inc=16)
            else:
                dq_dma[ci] = P.op("gpsimd", lambda e, out=out, cdst=cdst, row=row,
                                  wdt=wdt, col0=col0: e.dma_start(
                                      out=cdst.ap()[row:row + 128,
                                                    col0:col0 + wdt],
                                      in_=out),
                                  waits=[last], tag=gDQ[ci % 2], inc=16)

        exp_all = {}
        pv_all = {}
        ev_done = [Z] * NITS
        tr_done = [Z] * NITS
        pe_head_end = [Z] * NH
        dq_i = 0

        def plan_it_prologue(it):
            nonlocal dq_i
            h, qt = it // QT, it % QT
            if qt == 0 and h >= 1 and h + 1 < NH:
                plan_kv_load(h + 1, pe_head_end[h - 1])
            for _ in range(3):
                if dq_i < n_dq:
                    plan_dq_chunk(dq_i)
                    dq_i += 1

        def plan_S_exp(G):
            it, k = G // NG, G % NG
            h, qt = it // QT, it % QT
            g0, gl = groups[k]
            stX = stA if G % 2 == 0 else stB
            w = [kv_load[h][0]]
            if G >= 2:
                w.append(exp_all[G - 2])
            else:
                w.append(scT_done)
                w.append(bo_evac)
            smm = None
            for j in range(gl):
                kt = g0 + j
                smm = P.op("tensor", lambda e, stX=stX, j=j, kt=kt, h=h,
                           qt=qt: e.matmul(
                               stX[:, j * 512:j * 512 + QTW],
                               K_buf[h % 2][:, kt * 128:(kt + 1) * 128],
                               q_fm[:, h * TC + qt * QTW:
                                    h * TC + (qt + 1) * QTW],
                               start=True, stop=True),
                           waits=(w if j == 0 else []), tag=sPE)
            wE = [smm]
            if G >= 2:
                wE.append(pv_all[G - 2])
            exp_all[G] = P.op("scalar", lambda e, stX=stX, gl=gl, G=G:
                              e.activation(
                                  Pslab[G % 2][:, 0:gl * QTW].rearrange(
                                      "p (j q) -> p j q", q=QTW),
                                  stX[:, 0:1536].rearrange(
                                      "p (j q) -> p j q", q=512)
                                  [:, 0:gl, 0:QTW],
                                  AF.Exp, bias=c10_col[:, :], scale=SCALE_QK),
                              waits=wE, tag=sACT)

        def plan_PV(G):
            it, k = G // NG, G % NG
            h, qt = it // QT, it % QT
            buf = h % 2
            g0, gl = groups[k]
            wP = [exp_all[G], kv_load[h][1]]
            if k == 0 and it > 0:
                wP.append(ev_done[it - 1])
            pmm = None
            for j in range(gl):
                kt = g0 + j
                for qs in range(NQS):
                    off = (qs // 2) * 512 + (qs % 2) * 136
                    st = (kt == 0 and qs % 2 == 0)
                    sp = (kt == KT - 1 and (qs % 2 == 1 or qs == NQS - 1))
                    pmm = P.op("tensor",
                               lambda e, G=G, j=j, kt=kt, qs=qs, off=off,
                               buf=buf, st=st, sp=sp: e.matmul(
                                   ctxAB[:, off:off + 129],
                                   Pslab[G % 2][:, j * QTW + qs * 128:
                                                j * QTW + (qs + 1) * 128],
                                   V_buf[buf][:, kt * 129:(kt + 1) * 129],
                                   start=st, stop=sp),
                               waits=(wP if (j == 0 and qs == 0) else []),
                               tag=sPE)
            pv_all[G] = pmm
            if k == NG - 1:
                pe_head_end[h] = pmm

        def plan_evac(it):
            h, qt = it // QT, it % QT
            for qs in range(NQS):
                off = (qs // 2) * 512 + (qs % 2) * 136
                ci = (it % 2) * NQS + qs
                w = [pv_all[it * NG + NG - 1], sDVE.ms()]
                if it >= 2:
                    w.append(tr_done[it - 2])
                P.op("vector", lambda e, off=off, ci=ci: e.reciprocal(
                    rcp[:, ci:ci + 1], ctxAB[:, off + 128:off + 129]),
                    waits=w, tag=sDVE)
                ev_done[it] = P.op("vector", lambda e, off=off, ci=ci:
                                   e.tensor_scalar(
                                       ctx_n[ci][:, :], ctxAB[:, off:off + 128],
                                       rcp[:, ci:ci + 1], None, ALU.mult),
                                   waits=[sDVE.ms()], tag=sDVE)
            for qs in range(NQS):
                ci = (it % 2) * NQS + qs
                col = h * TC + qt * QTW + qs * 128
                tr_done[it] = P.op("sync", lambda e, ci=ci, col=col:
                                   e.dma_start_transpose(
                                       ctx_fm[:, col:col + 128], ctx_n[ci][:, :]),
                                   waits=[ev_done[it]], tag=sCT[it % 2], inc=16)

        TOT_G = NITS * NG
        for G in range(TOT_G):
            it, k = G // NG, G % NG
            if k == 0:
                if G >= 1:
                    plan_PV(G - 1)
                    plan_evac((G - 1) // NG)
                plan_it_prologue(it)
                plan_S_exp(G)
            else:
                plan_S_exp(G)
                plan_PV(G - 1)
        plan_PV(TOT_G - 1)
        plan_evac(NITS - 1)
        while dq_i < n_dq:
            plan_dq_chunk(dq_i)
            dq_i += 1

        P.op("gpsimd", lambda e: e.collective_compute(
            "AllGather", ALU.bypass, replica_groups=rg,
            ins=[cc_gu_in.ap().opt()], outs=[cc_gug.ap().opt()]),
            waits=[gDQ[0].ms(), gDQ[1].ms()], tag=cGU)
        P.op("gpsimd", lambda e: e.collective_compute(
            "AllGather", ALU.bypass, replica_groups=rg,
            ins=[cc_dn_in.ap().opt()], outs=[cc_dng.ap().opt()]),
            waits=[], tag=cDN)

        es_att.close()
        es_q.close()
        es_deq.close()

        es_ffn0 = ExitStack()
        gsl0 = es_ffn0.enter_context(nc.sbuf_tensor("gsl0", [128, 16 * 512],
                                                    F16))
        es_xl2 = ExitStack()
        xl2_fm = es_xl2.enter_context(nc.sbuf_tensor("xl2_fm", [128, 16 * TC],
                                                     BF16))
        es_p3 = ExitStack()
        wo_cbr = [es_p3.enter_context(nc.sbuf_tensor(f"wocb{i}", [128, 16 * 512],
                                                     BF16)) for i in range(1, 4)]
        wo_cb = [wo_cb0] + wo_cbr
        x1t3 = [es_p3.enter_context(nc.sbuf_tensor(f"x1p{i}", [128, D], F32))
                for i in range(2)]
        xbuf2 = [es_p3.enter_context(nc.sbuf_tensor(f"xb2{i}", [128, D], F32))
                 for i in range(2)]
        xl2_tok = [es_p3.enter_context(nc.sbuf_tensor(f"xl2t{i}", [128, D], BF16))
                   for i in range(2)]
        sums2 = es_p3.enter_context(nc.sbuf_tensor("sums2", [128, NT], F32))
        ssq2 = es_p3.enter_context(nc.sbuf_tensor("ssq2", [128, NT], F32))
        mcol2 = es_p3.enter_context(nc.sbuf_tensor("mcol2", [128, NT], F32))
        rstd2 = es_p3.enter_context(nc.sbuf_tensor("rstd2", [128, NT], F32))
        sq_scr2 = es_p3.enter_context(nc.sbuf_tensor("sq_scr2", [128, D], F16))

        TR4 = {0: [stA_bf[:, 0:128], stA_bf[:, 1024:1152],
                   stA_bf[:, 2048:2176], stB_bf[:, 0:128]],
               1: [stB_bf[:, 1024:1152], stB_bf[:, 2048:2176],
                   ctx_bf[:, 0:128], ctx_bf[:, 1024:1152]]}

        att_done = [sPE.ms(), sDVE.ms(), sACT.ms(), sCT[0].ms(), sCT[1].ms(),
                    gDQ[0].ms(), gDQ[1].ms()]

        gap = cc_gug.ap().rearrange("(r c p) o -> r c p o", c=16, p=128)

        def plan_gu_slab(dst, oglob0, extra, sem):
            first = True
            for r in range(NCORES):
                lo = max(oglob0, r * OSH)
                hi = min(oglob0 + 512, (r + 1) * OSH)
                if lo >= hi:
                    continue
                w = [(cGU, 1)] + (list(extra) if first else [])
                first = False
                l4, h4 = (lo - r * OSH) // 4, (hi - r * OSH) // 4
                d4 = (lo - oglob0) // 4
                gv = gap[r, :, :, :].rearrange("c p (g k) -> c p g k", g=4)
                for g4 in range(4):
                    P.op("sync", lambda e, dst=dst, gv=gv, l4=l4, h4=h4, d4=d4,
                         g4=g4: e.dma_start(
                             out=dst[:, :].rearrange("p (c g k) -> p c g k",
                                                     c=16, g=4)
                             [:, :, g4, d4:d4 + (h4 - l4)],
                             in_=gv[:, :, g4, l4:h4].transpose([1, 0, 2])),
                         waits=w, tag=sem, inc=16)
                    w = []

        plan_gu_slab(gsl0, 0, list(att_done), sGS[0])
        wo_ready = {0: wo0_ready}
        for dch in range(1, 4):
            wo_ready[dch] = P.op("gpsimd", lambda e, dch=dch: e.dma_start(
                out=wo_cb[dch][:, :].rearrange("p (c o) -> p c o", c=16),
                in_=wo_d.ap()[:, dch * 512:(dch + 1) * 512]
                .rearrange("(c p) o -> p c o", p=128)),
                waits=list(att_done), tag=gWO[dch], inc=16)

        op_mm = {}
        op_evac = {}
        x2_done = [Z] * NT
        ln2_apply = [Z] * NT
        spill = [Z] * NT
        xc2_done = [Z] * NT
        xl2_evac = {}
        xl2_evac_last = [Z] * NT

        def plan_tr(tt):
            par = tt % 2
            for c in range(16):
                r = c % 4
                w2 = [ln2_apply[tt], op_evac[(tt, 3)]]
                if c >= 4:
                    w2.append(xl2_evac[(tt, c - 4)])
                b, g = c // 4, c % 4
                P.op("tensor", lambda e, tt=tt, b=b, g=g, r=r, par=par:
                     e.transpose(TR4[par][r][:, :],
                                 xl2_tok[tt % 2][:, 512 * b + g:
                                                 512 * b + g + 509:4],
                                 id_sb[:, :]), waits=w2, tag=sPE)
                xl2_evac[(tt, c)] = P.op("scalar", lambda e, tt=tt, c=c, r=r,
                                         par=par: e.copy(
                                             xl2_fm[:, c * TC + tt * 128:
                                                    c * TC + (tt + 1) * 128],
                                             TR4[par][r][:, :]),
                                         waits=[sPE.ms(), sACT.ms()], tag=sACT)
            xl2_evac_last[tt] = xl2_evac[(tt, 15)]

        for t in range(NT):
            par = t % 2
            for dch in range(4):
                reg = PS8[par * 4 + dch]
                w = [wo_ready[dch]]
                if t == 0 and dch == 0:
                    w += att_done
                if (t - 2, dch) in op_evac:
                    w.append(op_evac[(t - 2, dch)])
                if t >= 2:
                    w.append(xl2_evac_last[t - 2] if xl2_evac_last[t - 2] != Z
                             else Z)
                for hh in range(NH):
                    op_mm[(dch, t)] = P.op(
                        "tensor", lambda e, t=t, hh=hh, dch=dch, reg=reg:
                        e.matmul(
                            reg[:, 0:512],
                            ctx_fm[:, hh * TC + t * 128:hh * TC + (t + 1) * 128],
                            wo_cb[dch][:, hh * 512:(hh + 1) * 512],
                            start=(hh == 0), stop=(hh == NH - 1)),
                        waits=(w if hh == 0 else []), tag=sPE)
                we = [op_mm[(dch, t)], sDVE.ms()]
                if dch == 0 and t >= 2:
                    we += [gSP.ms(), xc2_done[t - 2], ln2_apply[t - 2]]
                op_evac[(t, dch)] = P.op(
                    "vector", lambda e, dch=dch, t=t, reg=reg, par=par:
                    e.tensor_tensor(
                        x1t3[par][:, dch * 512:(dch + 1) * 512],
                        reg[:, 0:512], bo_bcast[:, dch * 512:(dch + 1) * 512],
                        ALU.add),
                    waits=we, tag=sDVE)
            w = list(att_done) if t < 2 else [ln2_apply[t - 2]]
            x2_done[t] = P.op("sync", lambda e, t=t, par=par: e.dma_start(
                out=xbuf2[par][:, :], in_=x_d.ap()[t * 128:(t + 1) * 128, :]),
                waits=w, tag=sX[par], inc=16)
            add_done = P.op("vector", lambda e, t=t, par=par: e.tensor_tensor(
                x1t3[par][:, :], x1t3[par][:, :], xbuf2[par][:, :], ALU.add),
                waits=[x2_done[t], op_evac[(t, 3)], sDVE.ms()], tag=sDVE)
            spill[t] = P.op("gpsimd", lambda e, t=t, par=par: e.dma_start(
                out=x1_d.ap()[t * 128:(t + 1) * 128, :], in_=x1t3[par][:, :]),
                waits=[add_done], tag=gSP, inc=16)
            P.op("vector", lambda e, t=t, par=par: e.reduce_sum(
                sums2[:, t:t + 1], x1t3[par][:, :], axis=AX),
                waits=[add_done], tag=sDVE)
            P.op("vector", lambda e, t=t: e.tensor_scalar(
                mcol2[:, t:t + 1], sums2[:, t:t + 1], 1.0 / D, None, ALU.mult),
                waits=[sDVE.ms()], tag=sDVE)
            sq2 = P.op("scalar", lambda e, t=t, par=par: e.activation(
                sq_scr2[:, :], x1t3[par][:, :], AF.Square,
                accum_out=ssq2[:, t:t + 1]),
                waits=[add_done, sACT.ms()], tag=sACT)
            xc2_done[t] = sq2
            m2d = P.op("vector", lambda e, t=t: e.tensor_scalar(
                sums2[:, t:t + 1], mcol2[:, t:t + 1], mcol2[:, t:t + 1], None,
                ALU.mult), waits=[sDVE.ms()], tag=sDVE)
            vard = P.op("vector", lambda e, t=t: e.scalar_tensor_tensor(
                out=rstd2[:, t:t + 1], in0=ssq2[:, t:t + 1], scalar=1.0 / D,
                in1=sums2[:, t:t + 1], op0=ALU.mult, op1=ALU.subtract),
                waits=[m2d, sq2, sDVE.ms()], tag=sDVE)
            P.op("scalar", lambda e, t=t: e.activation(
                rstd2[:, t:t + 1], rstd2[:, t:t + 1], AF.Ln, bias=eps5_col[:, :]),
                waits=[vard, sACT.ms()], tag=sACT)
            r2 = P.op("scalar", lambda e, t=t: e.activation(
                rstd2[:, t:t + 1], rstd2[:, t:t + 1], AF.Exp, scale=-0.5),
                waits=[sACT.ms()], tag=sACT)
            w = [r2, sDVE.ms()]
            if t >= 2:
                w.append(xl2_evac_last[t - 2])
            ln2_apply[t] = P.op("vector", lambda e, t=t, par=par: e.tensor_scalar(
                xl2_tok[par][:, :], x1t3[par][:, :], mcol2[:, t:t + 1],
                rstd2[:, t:t + 1], ALU.subtract, ALU.mult), waits=w, tag=sDVE)
            if t >= 1:
                plan_tr(t - 1)
        plan_tr(NT - 1)

        es_p3.close()
        es_wo0.close()
        es_ctx.close()
        es_bo.close()

        es_h = ExitStack()
        h_buf = es_h.enter_context(nc.sbuf_tensor("h_buf", [128, NCH_DN * TC],
                                                  F16, side="right"))
        es_p4 = ExitStack()
        gsl = [gsl0, es_p4.enter_context(nc.sbuf_tensor("gsl1", [128, 16 * 512],
                                                        F16))]
        usl = [es_p4.enter_context(nc.sbuf_tensor(f"usl{i}", [128, 16 * 512],
                                                  F16)) for i in range(2)]
        silu_sb = [es_p4.enter_context(nc.sbuf_tensor(f"silu{i}", [128, 1024],
                                                      F16)) for i in range(2)]

        ph3_done = [sPE.ms(), sDVE.ms(), sACT.ms(), gSP.ms(), sX[0].ms(),
                    sX[1].ms(), gWO[0].ms(), gWO[1].ms()]

        pair_mm_u = {}
        pair_mm_g = {}
        silu_done = {}
        hmul_done = {}
        pair_pe_end = {}
        TCC = min(1024, TC)
        for Bp in range(NB_I):
            bi = Bp % 2
            pe_free = pair_pe_end.get(Bp - 2, Z)
            if Bp == 0:
                plan_gu_slab(usl[0], IDIM, list(ph3_done), sGS[0])
                slab_done = sGS[0].ms()
            else:
                if Bp == 1:
                    extra = list(ph3_done)
                else:
                    extra = [pe_free] if pe_free != Z else []
                plan_gu_slab(gsl[bi], 512 * Bp, extra, sGS[bi])
                plan_gu_slab(usl[bi], IDIM + 512 * Bp, [], sGS[bi])
                slab_done = sGS[bi].ms()
            for g in range(4):
                p = Bp * 4 + g
                par = p % 2
                if par == 0:
                    rg_g = [PS8[0], PS8[1]]
                    rg_u = [PS8[3], PS8[4]]
                    gate_span = stA[:, 0:1024]
                    up_span = stB[:, 0:1024]
                else:
                    rg_g = [PS8[6], PS8[7]]
                    rg_u = [PS8[2], PS8[5]]
                    gate_span = ctxAB[:, 0:1024]
                    up_span = None
                for ui, (slab, regs) in enumerate(((gsl[bi], rg_g),
                                                   (usl[bi], rg_u))):
                    for th in range(NTH):
                        for c in range(16):
                            w = []
                            if th == 0 and c == 0:
                                w = [slab_done]
                                if ui == 0:
                                    w.append(silu_done.get(p - 2, Z))
                                    if p == 0:
                                        w += ph3_done
                                else:
                                    w.append(hmul_done.get(p - 2, Z))
                            mm = P.op("tensor", lambda e, slab=slab, c=c, g=g,
                                      th=th, reg=regs[th % 2]: e.matmul(
                                          reg[:, 0:TH],
                                          slab[:, :].rearrange(
                                              "p (cc g4 k) -> p cc g4 k",
                                              cc=16, g4=4)[:, c, g, :],
                                          xl2_fm[:, c * TC + th * TH:
                                                 c * TC + (th + 1) * TH],
                                          start=(c == 0), stop=(c == 15)),
                                      waits=w, tag=sPE)
                            if ui == 0:
                                pair_mm_g[p] = mm
                            else:
                                pair_mm_u[p] = mm
                silu_done[p] = P.op("scalar", lambda e, gate_span=gate_span, p=p:
                                    e.activation(silu_sb[p % 2][:, 0:TCC],
                                                 gate_span[:, 0:TCC], AF.Silu),
                                    waits=[pair_mm_g[p],
                                           hmul_done.get(p - 2, Z),
                                           sACT.ms()], tag=sACT)
                if par == 0:
                    hmul_done[p] = P.op("vector", lambda e, p=p, up_span=up_span:
                                        e.scalar_tensor_tensor(
                                            out=h_buf[:, p * TC:p * TC + TCC],
                                            in0=up_span[:, 0:TCC], scalar=0.125,
                                            in1=silu_sb[p % 2][:, 0:TCC],
                                            op0=ALU.mult, op1=ALU.mult),
                                        waits=[pair_mm_u[p], silu_done[p],
                                               sDVE.ms()], tag=sDVE)
                else:
                    hm = Z
                    for th in range(NTH):
                        hm = P.op("vector", lambda e, p=p, th=th,
                                  reg=rg_u[th % 2]: e.scalar_tensor_tensor(
                                      out=h_buf[:, p * TC + th * TH:
                                                p * TC + (th + 1) * TH],
                                      in0=reg[:, 0:TH], scalar=0.125,
                                      in1=silu_sb[p % 2][:, th * TH:(th + 1) * TH],
                                      op0=ALU.mult, op1=ALU.mult),
                                  waits=[pair_mm_u[p], silu_done[p],
                                         sDVE.ms()], tag=sDVE)
                    hmul_done[p] = hm
                pair_pe_end[Bp] = pair_mm_u[p]

        es_p4.close()
        es_xl2.close()
        es_ffn0.close()

        es_p5 = ExitStack()
        wd_sb = [es_p5.enter_context(nc.sbuf_tensor(f"wd{i}",
                                                    [128, NCH_DN * DSH], F16))
                 for i in range(2)]
        y_sb = es_p5.enter_context(nc.sbuf_tensor("y_sb", [128, NT * D], BF16))
        out_sb = [es_p5.enter_context(nc.sbuf_tensor(f"osb{i}", [128, D], F32))
                  for i in range(2)]
        x1t = [es_p5.enter_context(nc.sbuf_tensor(f"x1t{i}", [128, D], F32))
               for i in range(3)]
        ssq3 = es_p5.enter_context(nc.sbuf_tensor("ssq3", [128, 2 * NT], F32))
        rstd3 = es_p5.enter_context(nc.sbuf_tensor("rstd3", [128, NT], F32))
        sq_scr3 = es_p5.enter_context(nc.sbuf_tensor("sq_scr3", [128, D // 2],
                                                     F16))

        ph4_done = [sPE.ms(), sDVE.ms(), sACT.ms(), sGS[0].ms(), sGS[1].ms()]
        dap = cc_dng.ap().rearrange("(r c p) o -> r c p o", c=NCH_DN, p=128)
        wd_done = [Z, Z]
        wd_free = {}
        ev_list = []
        row_last = [Z] * NT
        ev_i = 0
        for rp in range(NCORES):
            w = [(cDN, 1)] + (list(ph4_done) if rp < 2 else [])
            if rp >= 2:
                w.append(wd_free[rp - 2])
            wd_done[rp % 2] = P.op("sync", lambda e, rp=rp: e.dma_start(
                out=wd_sb[rp % 2][:, :].rearrange("p (c o) -> p c o", c=NCH_DN),
                in_=dap[rp, :, :, :].transpose([1, 0, 2])),
                waits=w, tag=sWD[rp % 2], inc=16)
            last_mm = Z
            for t in range(NT):
                nds = (DSH + 511) // 512
                for dsub in range(nds):
                    N = min(512, DSH - dsub * 512)
                    reg = PS8[ev_i % 4]
                    for c in range(NCH_DN):
                        w2 = []
                        if c == 0:
                            w2 = [wd_done[rp % 2]]
                            if ev_i == 0:
                                w2 += ph4_done
                            if ev_i >= 4:
                                w2.append(ev_list[ev_i - 4])
                        last_mm = P.op("tensor", lambda e, rp=rp, t=t, c=c,
                                       reg=reg, dsub=dsub, N=N: e.matmul(
                                           reg[:, 0:N],
                                           h_buf[:, c * TC + t * 128:
                                                 c * TC + (t + 1) * 128],
                                           wd_sb[rp % 2][:, :].rearrange(
                                               "p (cc o) -> p cc o", cc=NCH_DN)
                                           [:, c, dsub * 512:dsub * 512 + N],
                                           start=(c == 0),
                                           stop=(c == NCH_DN - 1)),
                                       waits=w2, tag=sPE)
                    ev = P.op("vector", lambda e, rp=rp, t=t, reg=reg, dsub=dsub,
                              N=N: e.tensor_copy(
                                  y_sb[:, t * D + rp * DSH + dsub * 512:
                                       t * D + rp * DSH + dsub * 512 + N],
                                  reg[:, 0:N]),
                              waits=[last_mm, sDVE.ms()], tag=sDVE)
                    ev_list.append(ev)
                    ev_i += 1
                    row_last[t] = ev
            wd_free[rp] = last_mm
        out_dma = [Z] * NT
        fin_l = [Z] * NT
        for t in range(NT):
            P.op("scalar", lambda e, t=t: e.activation(
                sq_scr3[:, :], y_sb[:, t * D:t * D + D // 2], AF.Square,
                accum_out=ssq3[:, 2 * t:2 * t + 1]),
                waits=[row_last[t], sACT.ms()], tag=sACT)
            sq_b = P.op("scalar", lambda e, t=t: e.activation(
                sq_scr3[:, :], y_sb[:, t * D + D // 2:(t + 1) * D], AF.Square,
                accum_out=ssq3[:, 2 * t + 1:2 * t + 2]),
                waits=[sACT.ms()], tag=sACT)
            sq_add = P.op("vector", lambda e, t=t: e.tensor_tensor(
                ssq3[:, 2 * t:2 * t + 1], ssq3[:, 2 * t:2 * t + 1],
                ssq3[:, 2 * t + 1:2 * t + 2], ALU.add),
                waits=[sq_b, sDVE.ms()], tag=sDVE)
            P.op("scalar", lambda e, t=t: e.activation(
                rstd3[:, t:t + 1], ssq3[:, 2 * t:2 * t + 1], AF.Ln,
                bias=eps6_col[:, :], scale=1.0 / D),
                waits=[sq_add, sACT.ms()], tag=sACT)
            r3 = P.op("scalar", lambda e, t=t: e.activation(
                rstd3[:, t:t + 1], rstd3[:, t:t + 1], AF.Exp, scale=-0.5),
                waits=[sACT.ms()], tag=sACT)
            w3 = [gSP.ms()] + (list(ph4_done) if t < 3 else [])
            if t >= 3:
                w3.append(fin_l[t - 3])
            x1r = P.op("sync", lambda e, t=t: e.dma_start(
                out=x1t[t % 3][:, :], in_=x1_d.ap()[t * 128:(t + 1) * 128, :]),
                waits=w3, tag=sXR[t % 3], inc=16)
            w4 = [r3, x1r, sDVE.ms()]
            if t >= 2:
                w4.append(out_dma[t - 2])
            fin_l[t] = P.op("vector", lambda e, t=t: e.scalar_tensor_tensor(
                out=out_sb[t % 2][:, :], in0=y_sb[:, t * D:(t + 1) * D],
                scalar=rstd3[:, t:t + 1], in1=x1t[t % 3][:, :],
                op0=ALU.mult, op1=ALU.add), waits=w4, tag=sDVE)
            out_dma[t] = P.op("sync", lambda e, t=t: e.dma_start(
                out=out_d.ap()[t * 128:(t + 1) * 128, :],
                in_=out_sb[t % 2][:, :]),
                waits=[fin_l[t]], tag=sOUT[t % 4], inc=16)
        P.op("sync", None, waits=[s.ms() for s in sOUT])

        P.emit(block)
        es_p5.close()
        es_h.close()

    es.close()
    return nc


_PROG = {}


def _get_program(ncores, tc):
    key = (ncores, tc)
    if key not in _PROG:
        _PROG[key] = build_program(ncores, tc)
    return _PROG[key]


def kernel(L0_attn_x, L0_attn_wq, L0_attn_bq, L0_attn_wk, L0_attn_bk,
           L0_attn_wv, L0_attn_bv, L0_attn_wo, L0_attn_bo,
           L0_ffn_gate_up_packed, L0_ffn_gate_up_scales,
           L0_ffn_down_packed, L0_ffn_down_scales):
    NCORES = 8
    xin = np.asarray(L0_attn_x)
    x = np.ascontiguousarray(xin.reshape(-1, D).astype(np.float32, copy=False))
    TC = x.shape[0] // NCORES
    nc = _get_program(NCORES, TC)
    gup = np.asarray(L0_ffn_gate_up_packed)
    gus = np.asarray(L0_ffn_gate_up_scales)
    dnp = np.asarray(L0_ffn_down_packed)
    dns = np.asarray(L0_ffn_down_scales)
    OSH = GO // NCORES
    DSH = D // NCORES
    wq = np.ascontiguousarray(np.asarray(L0_attn_wq))
    wk = np.ascontiguousarray(np.asarray(L0_attn_wk))
    wv = np.ascontiguousarray(np.asarray(L0_attn_wv))
    wo = np.ascontiguousarray(np.asarray(L0_attn_wo))
    bq = np.ascontiguousarray(np.asarray(L0_attn_bq))
    bk = np.ascontiguousarray(np.asarray(L0_attn_bk))
    bv = np.ascontiguousarray(np.asarray(L0_attn_bv))
    bo = np.ascontiguousarray(np.asarray(L0_attn_bo))
    in_maps = []
    for c in range(NCORES):
        in_maps.append({
            "x": np.ascontiguousarray(x[c * TC:(c + 1) * TC]),
            "wq": wq, "wk": wk, "wv": wv, "wo": wo,
            "bq": bq, "bk": bk, "bv": bv, "bo": bo,
            "gup": np.ascontiguousarray(gup[c * OSH:(c + 1) * OSH]),
            "gus": np.ascontiguousarray(gus[:, c * OSH:(c + 1) * OSH]),
            "dnp": np.ascontiguousarray(dnp[c * DSH:(c + 1) * DSH]),
            "dns": np.ascontiguousarray(dns[:, c * DSH:(c + 1) * DSH]),
        })
    res = run_bass_kernel_spmd(nc, in_maps, core_ids=list(range(NCORES)))
    out = np.concatenate([np.asarray(res.results[c]["out"])
                          for c in range(NCORES)], axis=0)
    return out.reshape(xin.shape).astype(np.float32, copy=False)
